# version 02
# baseline (speedup 1.0000x reference)
from contextlib import ExitStack

import numpy as np
import ml_dtypes
import concourse.bass as bass
import concourse.mybir as mybir
from concourse.bass_utils import run_bass_kernel_spmd

F32 = mybir.dt.float32
BF16 = mybir.dt.bfloat16
AF = mybir.ActivationFunctionType
ALU = mybir.AluOpType
AX = mybir.AxisListType

T, D, H = 4096, 2048, 8
NTT, TTK = 32, 128
NTC, TCK = 8, 512
NJT = 16
E, FF, NFT = 8, 1024, 8
VS = 264
SCALE = float(1.0 / np.sqrt(128.0))
EXPSHIFT = -11.5
LN_EPS = 1e-5
RMS_EPS = 1.1920929e-07
LAMBDA_INIT = 0.8


class Sem:
    def __init__(self, nc, name):
        self.h = nc.alloc_semaphore(name=name)
        self.n = 0

    def inc(self, instr, k=1):
        instr.then_inc(self.h, k)
        self.n += k
        return self.n


def build_program(debug_outputs=(), stop_after=None, a2_sub=None, skip_p0=False):
    nc = bass.Bass("TRN2", target_bir_lowering=False)

    def dk(name):
        return "ExternalOutput" if name in debug_outputs else "Internal"

    x_d = nc.dram_tensor("x", [T, D], F32, kind="ExternalInput").ap()
    wqkv_d = nc.dram_tensor("wqkvT", [H, 128, NJT * 768], F32, kind="ExternalInput").ap()
    wo_d = nc.dram_tensor("woT", [128, NJT * D], F32, kind="ExternalInput").ap()
    rw_d = nc.dram_tensor("rw", [128, NJT * 8], F32, kind="ExternalInput").ap()
    gup_d = nc.dram_tensor("gup", [E, NFT, 128, NJT * 256], F32, kind="ExternalInput").ap()
    dwn_d = nc.dram_tensor("dwn", [E, 2, 128, NFT * 1024], F32, kind="ExternalInput").ap()
    lam_d = nc.dram_tensor("lam_d", [1, 1024], F32, kind="ExternalInput").ap()
    masks_d = nc.dram_tensor("masks_d", [128, 4 * TCK], F32, kind="ExternalInput").ap()
    identf_d = nc.dram_tensor("identf_d", [128, 128], F32, kind="ExternalInput").ap()
    sel_d = nc.dram_tensor("sel_d", [8, 8 * 128], F32, kind="ExternalInput").ap()
    ones_d = nc.dram_tensor("ones32_d", [1, 128], F32, kind="ExternalInput").ap()

    out_d = nc.dram_tensor("out", [T, D], F32, kind="ExternalOutput").ap()

    ln1T_s = nc.dram_tensor("ln1T_s", [128, NJT, T], BF16, kind=dk("ln1T_s")).ap()
    yT_s = nc.dram_tensor("yT_s", [128, NJT, T], BF16, kind=dk("yT_s")).ap()
    ln2T_s = nc.dram_tensor("ln2T_s", [128, NJT, T], BF16, kind=dk("ln2T_s")).ap()
    x2_s = nc.dram_tensor("x2_s", [NTT, 128, D], F32, kind=dk("x2_s")).ap()
    gupb_s = nc.dram_tensor("gupb_s", [E, NFT, 128, NJT * 256], BF16, kind="Internal").ap()
    cwT_s = nc.dram_tensor("cwT_s", [8, T], F32, kind=dk("cwT_s")).ap()
    wqkvb_s = nc.dram_tensor("wqkvb_s", [H, 128, NJT * 768], BF16, kind=dk("wqkvb_s")).ap()
    wob_s = nc.dram_tensor("wob_s", [128, NJT * D], BF16, kind="Internal").ap()
    rwb_s = nc.dram_tensor("rwb_s", [128, NJT * 8], BF16, kind="Internal").ap()
    dwnb_s = nc.dram_tensor("dwnb_s", [E, 2, 128, NFT * 1024], BF16, kind="Internal").ap()
    cw_dbg = None
    if "cw" in debug_outputs:
        cw_dbg = nc.dram_tensor("cw", [128, NTT * 8], F32, kind="ExternalOutput").ap()

    identb = nc.alloc_sbuf_tensor("identb", [128, 128], BF16)
    identf = nc.alloc_sbuf_tensor("identf", [128, 128], F32)
    masks = nc.alloc_sbuf_tensor("masks", [128, 4 * TCK], BF16)
    sel = nc.alloc_sbuf_tensor("sel", [8, 8 * 128], F32)
    ones32 = nc.alloc_sbuf_tensor("ones32", [1, 128], F32)
    lam = nc.alloc_sbuf_tensor("lam", [1, 1024], F32)
    lprod = nc.alloc_sbuf_tensor("lprod", [1, 512], F32)
    lsc = nc.alloc_sbuf_tensor("lsc", [1, 8], F32)
    nlam = nc.alloc_sbuf_tensor("nlam", [128, 1], F32)
    bias_exp = nc.alloc_sbuf_tensor("bias_exp", [128, 1], F32)
    eps_ln = nc.alloc_sbuf_tensor("eps_ln", [128, 1], F32)
    eps_rms = nc.alloc_sbuf_tensor("eps_rms", [128, 1], F32)
    cw_sb = nc.alloc_sbuf_tensor("cw_sb", [128, NTT * 8], F32)
    stats = nc.alloc_sbuf_tensor("stats", [128, 16], F32)
    rmsst = nc.alloc_sbuf_tensor("rmsst", [128, 16], F32)

    sems = {}
    def S(name):
        if name not in sems:
            sems[name] = Sem(nc, name)
        return sems[name]

    (sCn, sP0l, sP0s, sX, sA0s, sA0v, sA0t, sL1W, sL1R, sWq, sQKt, sQKs, sQKv,
     sTt, sTv, sSTt, sSTs, sMv, sPVt, sYv, sYTt, sYTv, sYW, sYR, sOPt, sA2v,
     sA2s, sRt, sCWt, sCWv, sL2W, sX2W, sL2R, sX2R, sGW, sDW, sCBt, sCBv,
     sB1t, sB1s, sB1v, sB2t, sB2v, sOW, sLamv, sLams, sLamt, sCWw, sCTR, sYs) = [S(n) for n in (
        "sCn sP0l sP0s sX sA0s sA0v sA0t sL1W sL1R sWq sQKt sQKs sQKv sTt sTv "
        "sSTt sSTs sMv sPVt sYv sYTt sYTv sYW sYR sOPt sA2v sA2s sRt sCWt sCWv "
        "sL2W sX2W sL2R sX2R sGW sDW sCBt sCBv sB1t sB1s sB1v sB2t sB2v sOW "
        "sLamv sLams sLamt sCWw sCTR sYs").split()]

    tv, vv, sv, gv, yv = nc.tensor, nc.vector, nc.scalar, nc.gpsimd, nc.sync

    sCn.inc(yv.dma_start(identf[:, :], identf_d), 16)
    sCn.inc(yv.dma_start(sel[:, :], sel_d), 16)
    sCn.inc(yv.dma_start(ones32[:, :], ones_d), 16)
    sCn.inc(yv.dma_start(lam[:, :], lam_d), 16)
    sCn.inc(gv.memset(bias_exp[:, :], EXPSHIFT), 1)
    sCn.inc(gv.memset(eps_ln[:, :], LN_EPS), 1)
    sCn.inc(gv.memset(eps_rms[:, :], RMS_EPS), 1)
    sCn.inc(gv.memset(stats[:, :], 0.0), 1)
    sCn.inc(gv.memset(rmsst[:, :], 0.0), 1)
    with nc.sbuf_tensor("mstage", [128, 4 * TCK], F32) as mstage:
        ml_tok = Sem(nc, "sMst")
        ml_tok.inc(yv.dma_start(mstage[:, :], masks_d), 16)
        vv.wait_ge(ml_tok.h, ml_tok.n)
        sCn.inc(vv.tensor_copy(masks[:, :], mstage[:, :]), 1)
        vv.wait_ge(sCn.h, 16)
        sCn.inc(vv.tensor_copy(identb[:, :], identf[:, :]), 1)
    cn_target = sCn.n

    with nc.psum_tensor("plam", [128, 1], F32) as plam:
        vv.wait_ge(sCn.h, cn_target)
        t1 = sLamv.inc(vv.tensor_mul(lprod[0:1, 0:256], lam[0:1, 0:256], lam[0:1, 256:512]), 1)
        t2 = sLamv.inc(vv.tensor_mul(lprod[0:1, 256:512], lam[0:1, 512:768], lam[0:1, 768:1024]), 1)
        vv.wait_ge(sLamv.h, t2)
        sLamv.inc(vv.reduce_sum(lsc[0:1, 0:1], lprod[0:1, 0:256], axis=AX.X), 1)
        sLamv.inc(vv.reduce_sum(lsc[0:1, 1:2], lprod[0:1, 256:512], axis=AX.X), 1)
        sv.wait_ge(sLamv.h, sLamv.n)
        sLams.inc(sv.activation(lsc[0:1, 4:6], lsc[0:1, 0:2], AF.Exp), 1)
        vv.wait_ge(sLams.h, sLams.n)
        t3 = sLamv.inc(vv.tensor_sub(lsc[0:1, 6:7], lsc[0:1, 5:6], lsc[0:1, 4:5]), 1)
        vv.wait_ge(sLamv.h, t3)
        sLamv.inc(vv.tensor_scalar(lsc[0:1, 6:7], lsc[0:1, 6:7], -LAMBDA_INIT, None, op0=ALU.add), 1)
        tv.wait_ge(sLamv.h, sLamv.n)
        sLamt.inc(tv.matmul(plam[:, :], ones32[0:1, :], lsc[0:1, 6:7], start=True, stop=True), 1)
        vv.wait_ge(sLamt.h, sLamt.n)
        sLamv.inc(vv.tensor_copy(nlam[:, :], plam[:, :]), 1)
    lam_v_final = sLamv.n

    JW = 4096
    p0_jobs = []
    for h in range(H):
        for c in range(3):
            p0_jobs.append((wqkv_d[h][:, c * JW:(c + 1) * JW], wqkvb_s[h][:, c * JW:(c + 1) * JW]))
    p0_wq_job = [3 * (h + 1) for h in range(H)]
    for c in range(8):
        p0_jobs.append((wo_d[:, c * JW:(c + 1) * JW], wob_s[:, c * JW:(c + 1) * JW]))
    p0_jobs.append((rw_d, rwb_s))
    p0_attn_jobs = len(p0_jobs)
    for e in range(E):
        for ft in range(NFT):
            p0_jobs.append((gup_d[e, ft], gupb_s[e, ft]))
    for e in range(E):
        for dh in range(2):
            for c in range(2):
                p0_jobs.append((dwn_d[e, dh][:, c * JW:(c + 1) * JW],
                                dwnb_s[e, dh][:, c * JW:(c + 1) * JW]))
    p0_ctx = ExitStack()
    p0_f32 = p0_ctx.enter_context(nc.sbuf_tensor("p0f", [128, JW], F32))
    p0_bf = p0_ctx.enter_context(nc.sbuf_tensor("p0b", [128, JW], BF16))
    sP0c = Sem(nc, "sP0c")
    p0_state = {"i": 0}

    def emit_p0(k):
        if skip_p0:
            return
        for _ in range(k):
            i = p0_state["i"]
            if i >= len(p0_jobs):
                return
            srcap, dstap = p0_jobs[i]
            w = srcap.shape[-1]
            if i == 0:
                yv.wait_ge(sCn.h, cn_target)
            if i >= 1:
                yv.wait_ge(sP0c.h, i)
            sP0l.inc(yv.dma_start(p0_f32[:, 0:w], srcap), 16)
            sv.wait_ge(sP0l.h, sP0l.n)
            if i >= 1:
                sv.wait_ge(sP0s.h, 16 * i)
            sP0c.inc(sv.copy(p0_bf[:, 0:w], p0_f32[:, 0:w]), 1)
            yv.wait_ge(sP0c.h, sP0c.n)
            sP0s.inc(yv.dma_start(dstap, p0_bf[:, 0:w]), 16)
            p0_state["i"] += 1

    def finish_early():
        vv.wait_ge(sCn.h, cn_target)
        dummy = sOW
        fi = yv.dma_start(out_d[0:128, 0:128], identf[:, :])
        dummy.inc(fi, 16)
        yv.wait_ge(dummy.h, dummy.n)

    if stop_after == "init":
        emit_p0(len(p0_jobs))
        gv.wait_ge(sP0s.h, (0 if skip_p0 else 16 * len(p0_jobs)))
        finish_early()
        return nc

    with (
        nc.sbuf_tensor("xb", [128, 2 * D], F32) as xb,
        nc.sbuf_tensor("sqs", [128, D], F32) as sqs,
        nc.sbuf_tensor("ln1b", [128, D], BF16) as ln1b,
        nc.sbuf_tensor("stage0", [128, NJT * TCK], BF16) as stage0,
        nc.psum_tensor("pt0", [128, 2 * 1024], BF16) as pt0,
    ):
        emit_p0(12)
        sv.wait_ge(sCn.h, cn_target)
        vv.wait_ge(sCn.h, cn_target)
        tv.wait_ge(sCn.h, cn_target)
        norm_tok = [0] * NTT
        tr_toks = [0] * NTT
        evac_tok = [0, 0]
        flush_tok = 0
        for tt in range(NTT):
            slx = tt % 2
            if tt >= 2:
                yv.wait_ge(sA0v.h, norm_tok[tt - 2])
            sX.inc(yv.dma_start(xb[:, slx * D:(slx + 1) * D], x_d[tt * TTK:(tt + 1) * TTK, :]), 16)
            x_need = sX.n
            xs = xb[:, slx * D:(slx + 1) * D]
            st = stats[:, (tt % 2) * 8:(tt % 2) * 8 + 8]
            sv.wait_ge(sX.h, x_need)
            if tt >= 1:
                sv.wait_ge(sA0v.h, sA0v.n)
            ssq_tok = sA0s.inc(sv.activation(sqs[:, :], xs, AF.Square, accum_out=st[:, 0:1]), 1)
            vv.wait_ge(sX.h, x_need)
            t_r = sA0v.inc(vv.reduce_sum(st[:, 1:2], xs, axis=AX.X), 1)
            vv.wait_ge(sA0s.h, ssq_tok)
            vv.wait_ge(sA0v.h, t_r)
            t_m = sA0v.inc(vv.tensor_scalar(st[:, 2:3], st[:, 1:2], 1.0 / D, None, op0=ALU.mult), 1)
            sA0v.inc(vv.tensor_scalar(st[:, 3:4], st[:, 0:1], 1.0 / D, None, op0=ALU.mult), 1)
            vv.wait_ge(sA0v.h, t_m)
            t_n = sA0v.inc(vv.tensor_scalar(st[:, 4:5], st[:, 2:3], st[:, 2:3], -1.0, op0=ALU.mult, op1=ALU.mult), 1)
            vv.wait_ge(sA0v.h, t_n)
            var_tok = sA0v.inc(vv.tensor_add(st[:, 3:4], st[:, 3:4], st[:, 4:5]), 1)
            sv.wait_ge(sA0v.h, var_tok)
            sA0s.inc(sv.activation(st[:, 5:6], st[:, 3:4], AF.Ln, bias=eps_ln[:, 0:1]), 1)
            rstd_tok = sA0s.inc(sv.activation(st[:, 6:7], st[:, 5:6], AF.Exp, scale=-0.5), 1)
            vv.wait_ge(sA0s.h, rstd_tok)
            if tt >= 1:
                vv.wait_ge(sA0t.h, tr_toks[tt - 1])
            norm_tok[tt] = sA0v.inc(vv.tensor_scalar(ln1b[:, :], xs, st[:, 2:3], st[:, 6:7],
                                                     op0=ALU.subtract, op1=ALU.mult), 1)
            tv.wait_ge(sA0v.h, norm_tok[tt])
            for b in range(2):
                if evac_tok[b]:
                    tv.wait_ge(sA0v.h, evac_tok[b])
                ti = None
                for k in range(8):
                    jt = b * 8 + k
                    ti = tv.transpose(pt0[:, b * 1024 + k * 128:b * 1024 + (k + 1) * 128],
                                      ln1b[:, jt * 128:(jt + 1) * 128], identb[:, :])
                tr_toks[tt] = sA0t.inc(ti, 1)
                vv.wait_ge(sA0t.h, tr_toks[tt])
                if tt % 4 == 0 and b == 0 and flush_tok:
                    vv.wait_ge(sL1W.h, flush_tok)
                dst = stage0[:, :].rearrange("p (j q c) -> p j q c", q=4, c=128)[:, b * 8:(b + 1) * 8, tt % 4, :]
                src = pt0[:, b * 1024:(b + 1) * 1024].rearrange("p (j c) -> p j c", c=128)
                evac_tok[b] = sA0v.inc(vv.tensor_copy(dst, src), 1)
            if tt % 4 == 3:
                tc = tt // 4
                yv.wait_ge(sA0v.h, sA0v.n)
                for half in range(4):
                    src = stage0[:, :].rearrange("p (j c) -> p j c", c=TCK)[:, half * 4:(half + 1) * 4, :]
                    sL1W.inc(yv.dma_start(ln1T_s[:, half * 4:(half + 1) * 4, tc * TCK:(tc + 1) * TCK], src), 16)
                flush_tok = sL1W.n

    if stop_after == "a0":
        yv.wait_ge(sL1W.h, sL1W.n)
        finish_early()
        return nc

    with (
        nc.sbuf_tensor("l1b", [128, 2 * NJT * TCK], BF16) as l1b,
        nc.sbuf_tensor("wqb", [128, 2 * NJT * 768], BF16) as wqb,
        nc.sbuf_tensor("qnT1", [128, T], BF16) as qnT1,
        nc.sbuf_tensor("qnT2", [128, T], BF16) as qnT2,
        nc.sbuf_tensor("knT1", [128, T], BF16) as knT1,
        nc.sbuf_tensor("knT2", [128, T], BF16) as knT2,
        nc.sbuf_tensor("vp", [128, NTT * VS], BF16) as vp,
        nc.sbuf_tensor("pbuf", [128, 4 * TCK], BF16) as pbuf,
        nc.sbuf_tensor("qkst", [128, TCK], BF16) as qkst,
        nc.sbuf_tensor("sqs1", [128, TCK], F32) as sqs1,
        nc.sbuf_tensor("ytmp", [128, 4 * 256], F32) as ytmp,
        nc.sbuf_tensor("ybf", [128, 4 * 256], BF16) as ybf,
        nc.sbuf_tensor("yst", [128, 2 * TCK], BF16) as yst,
        nc.sbuf_tensor("zz", [128, 16], F32) as zz,
    ):
        vones = vp[:, :].rearrange("p (k c) -> p k c", c=VS)[:, :, 256:257]
        vv.wait_ge(sCn.h, cn_target)
        sCn.inc(vv.memset(vones, 1.0), 1)
        vones_tok = sCn.n

        slab_tok = [0, 0]
        wq_tok = [0, 0]
        if not skip_p0:
            yv.wait_ge(sP0s.h, 16 * p0_wq_job[0])
        sWq.inc(yv.dma_start(wqb[:, 0:NJT * 768], wqkvb_s[0]), 16)
        l1_iter = 0
        prev_ev = [0, 0, 0, 0]
        pb_tok = [0, 0, 0, 0]
        pst_tok = [0, 0, 0]
        yst_tok = 0
        ytr_prev = 0

        for h in range(H):
            with (
                nc.psum_tensor(f"pq{h}", [128, 2 * 1024], F32) as pq,
                nc.psum_tensor(f"ptq{h}", [128, 512], BF16) as ptq,
            ):
                if h + 1 < H:
                    if wq_tok[(h + 1) % 2]:
                        yv.wait_ge(sQKt.h, wq_tok[(h + 1) % 2])
                    if not skip_p0:
                        yv.wait_ge(sP0s.h, 16 * p0_wq_job[h + 1])
                    sWq.inc(yv.dma_start(wqb[:, ((h + 1) % 2) * NJT * 768:((h + 1) % 2 + 1) * NJT * 768],
                                         wqkvb_s[h + 1]), 16)
                wq_need = 16 * (h + 1)
                emit_p0(8)
                if h >= 1:
                    tv.wait_ge(sSTs.h, sSTs.n)
                    tv.wait_ge(sYv.h, sYv.n)
                    tv.wait_ge(sYTv.h, sYTv.n)
                    vv.wait_ge(sSTt.h, sSTt.n)
                    vv.wait_ge(sPVt.h, sPVt.n)
                pq_s_tok = [0, 0]
                pq_v_tok = [0, 0]
                ptq_tok = 0
                qk_ev_tok = [0] * NTT
                for tc in range(NTC):
                    sl = l1_iter % 2
                    if l1_iter >= 2:
                        yv.wait_ge(sQKt.h, slab_tok[sl])
                    if l1_iter == 0:
                        yv.wait_ge(sL1W.h, sL1W.n)
                    for qtr in range(4):
                        src = ln1T_s[:, qtr * 4:(qtr + 1) * 4, tc * TCK:(tc + 1) * TCK]
                        dst = l1b[:, sl * NJT * TCK:(sl + 1) * NJT * TCK].rearrange(
                            "p (j c) -> p j c", c=TCK)[:, qtr * 4:(qtr + 1) * 4, :]
                        sL1R.inc(yv.dma_start(dst, src), 16)
                    slab_need = sL1R.n
                    l1_iter += 1
                    qk_tok = 0
                    for tl in range(4):
                        tt = tc * 4 + tl
                        ps = tt % 2
                        tv.wait_ge(sL1R.h, slab_need)
                        tv.wait_ge(sWq.h, wq_need)
                        if pq_s_tok[ps]:
                            tv.wait_ge(sQKs.h, pq_s_tok[ps])
                        if pq_v_tok[ps]:
                            tv.wait_ge(sQKv.h, pq_v_tok[ps])
                        mm = None
                        for j in range(NJT):
                            lhsT = l1b[:, sl * NJT * TCK + j * TCK + tl * 128:
                                       sl * NJT * TCK + j * TCK + (tl + 1) * 128]
                            rhs = wqb[:, (h % 2) * NJT * 768 + j * 768:(h % 2) * NJT * 768 + (j + 1) * 768]
                            st0 = (j == 0)
                            tv.matmul(pq[:, ps * 1024:ps * 1024 + 512], lhsT, rhs[:, 0:512],
                                      start=st0, stop=(j == NJT - 1))
                            mm = tv.matmul(pq[:, ps * 1024 + 512:ps * 1024 + 768], lhsT, rhs[:, 512:768],
                                           start=st0, stop=(j == NJT - 1))
                        qk_tok = sQKt.inc(mm, 1)
                        sv.wait_ge(sQKt.h, qk_tok)
                        if tt >= 1:
                            sv.wait_ge(sQKv.h, qk_ev_tok[tt - 1])
                        ai = None
                        for g in range(4):
                            ai = sv.activation(sqs1[:, g * 128:(g + 1) * 128],
                                               pq[:, ps * 1024 + g * 128:ps * 1024 + (g + 1) * 128],
                                               AF.Square, accum_out=rmsst[:, g:g + 1])
                        pq_s_tok[ps] = sA = sQKs.inc(ai, 1)
                        rcol = 8 + (tt % 2) * 4
                        sQKs.inc(sv.activation(rmsst[:, 4:8], rmsst[:, 0:4], AF.Ln,
                                               bias=eps_rms[:, 0:1], scale=1.0 / 128.0), 1)
                        rstd_tok = sQKs.inc(sv.activation(rmsst[:, rcol:rcol + 4], rmsst[:, 4:8], AF.Exp, scale=-0.5), 1)
                        vv.wait_ge(sQKs.h, rstd_tok)
                        if ptq_tok:
                            vv.wait_ge(sTt.h, sTt.n)
                        ev = None
                        for g in range(4):
                            ev = vv.tensor_scalar(qkst[:, g * 128:(g + 1) * 128],
                                                  pq[:, ps * 1024 + g * 128:ps * 1024 + (g + 1) * 128],
                                                  rmsst[:, rcol + g:rcol + g + 1], None, op0=ALU.mult)
                        qn_tok = sQKv.inc(ev, 1)
                        vv1 = vv.tensor_copy(vp[:, tt * VS:tt * VS + 256], pq[:, ps * 1024 + 512:ps * 1024 + 768])
                        pq_v_tok[ps] = sQKv.inc(vv1, 1)
                        tv.wait_ge(sQKv.h, qn_tok)
                        if ptq_tok:
                            tv.wait_ge(sQKv.h, ptq_tok)
                        ti = None
                        for g in range(4):
                            ti = tv.transpose(ptq[:, g * 128:(g + 1) * 128],
                                              qkst[:, g * 128:(g + 1) * 128], identb[:, :])
                        tr_tok = sTt.inc(ti, 1)
                        vv.wait_ge(sTt.h, tr_tok)
                        e4 = None
                        for gi, dstT in enumerate((qnT1, qnT2, knT1, knT2)):
                            e4 = vv.tensor_copy(dstT[:, tt * 128:(tt + 1) * 128],
                                                ptq[:, gi * 128:(gi + 1) * 128])
                        ptq_tok = sQKv.inc(e4, 1)
                        qk_ev_tok[tt] = ptq_tok
                    slab_tok[sl] = qk_tok
                if h + 1 < H:
                    wq_tok[(h + 1) % 2] = sQKt.n
                wq_tok[h % 2] = sQKt.n
                qkv_v_final = sQKv.n
                qkv_s_final = sQKs.n

            with (
                nc.psum_tensor(f"pst{h}", [128, 3 * TCK], F32) as pst,
                nc.psum_tensor(f"py{h}", [128, 4 * 512], F32) as py,
                nc.psum_tensor(f"pty{h}", [128, 256], BF16) as pty,
            ):
                if h == 0:
                    tv.wait_ge(sCn.h, vones_tok)
                kctr = 0
                for qsc in range(NTC):
                    nkt = 4 * qsc + 4
                    evq = [0, 0, 0, 0]
                    for var in range(2):
                        qn = qnT1 if var == 0 else qnT2
                        kn = knT1 if var == 0 else knT2
                        tv.wait_ge(sQKv.h, qkv_v_final)
                        tv.wait_ge(sQKs.h, qkv_s_final)

                        pend = []

                        def flush_pv(item, first_base, last_kt):
                            kt, pslot, ex_tok, mask_tok = item
                            if mask_tok:
                                tv.wait_ge(sMv.h, mask_tok)
                            else:
                                tv.wait_ge(sSTs.h, ex_tok)
                            mm = None
                            for qt in range(4):
                                if kt == 0 and first_base[qt]:
                                    tv.wait_ge(sYv.h, first_base[qt])
                                mm = tv.matmul(py[:, qt * 512:qt * 512 + 257],
                                               pbuf[:, pslot * TCK + qt * 128:pslot * TCK + (qt + 1) * 128],
                                               vp[:, kt * VS:kt * VS + 257],
                                               start=(kt == 0), stop=(kt == last_kt))
                            pb_tok[pslot] = sPVt.inc(mm, 1)

                        for kt in range(nkt):
                            b = kctr % 3
                            if pst_tok[b]:
                                tv.wait_ge(sSTs.h, pst_tok[b])
                            mm = tv.matmul(pst[:, b * TCK:(b + 1) * TCK],
                                           kn[:, kt * 128:(kt + 1) * 128],
                                           qn[:, qsc * TCK:(qsc + 1) * TCK],
                                           start=True, stop=True)
                            st_tok = sSTt.inc(mm, 1)
                            pslot = kctr % 4
                            sv.wait_ge(sSTt.h, st_tok)
                            if pb_tok[pslot]:
                                sv.wait_ge(sPVt.h, pb_tok[pslot])
                            ex = sv.activation(pbuf[:, pslot * TCK:(pslot + 1) * TCK],
                                               pst[:, b * TCK:(b + 1) * TCK],
                                               AF.Exp, bias=bias_exp[:, 0:1], scale=SCALE)
                            ex_tok = sSTs.inc(ex, 1)
                            pst_tok[b] = ex_tok
                            o = kt - 4 * qsc
                            mask_tok = None
                            if o >= 0:
                                vv.wait_ge(sSTs.h, ex_tok)
                                if pb_tok[pslot]:
                                    vv.wait_ge(sPVt.h, pb_tok[pslot])
                                mk = vv.tensor_mul(pbuf[:, pslot * TCK:(pslot + 1) * TCK],
                                                   pbuf[:, pslot * TCK:(pslot + 1) * TCK],
                                                   masks[:, o * TCK:(o + 1) * TCK])
                                mask_tok = sMv.inc(mk, 1)
                            pend.append((kt, pslot, ex_tok, mask_tok))
                            kctr += 1
                            if len(pend) > 2:
                                flush_pv(pend.pop(0), prev_ev if var == 0 else evq, nkt - 1)
                        while pend:
                            flush_pv(pend.pop(0), prev_ev if var == 0 else evq, nkt - 1)

                        pv_done = sPVt.n
                        sv.wait_ge(sPVt.h, pv_done)
                        if var == 0:
                            zt = None
                            for qt in range(4):
                                sv.activation(zz[:, 8:9], py[:, qt * 512 + 256:qt * 512 + 257], AF.Ln)
                                zt = sv.activation(zz[:, qt:qt + 1], zz[:, 8:9], AF.Exp, scale=-1.0)
                            z_tok = sYs.inc(zt, 1)
                            vv.wait_ge(sYs.h, z_tok)
                            vv.wait_ge(sPVt.h, pv_done)
                            for qt in range(4):
                                e = vv.tensor_scalar(ytmp[:, qt * 256:(qt + 1) * 256],
                                                     py[:, qt * 512:qt * 512 + 256],
                                                     zz[:, qt:qt + 1], None, op0=ALU.mult)
                                evq[qt] = sYv.inc(e, 1)
                        else:
                            zt = None
                            for qt in range(4):
                                sv.activation(zz[:, 8:9], py[:, qt * 512 + 256:qt * 512 + 257], AF.Ln)
                                sv.activation(zz[:, 9:10], zz[:, 8:9], AF.Exp, scale=-1.0)
                                zt = sv.mul(zz[:, 4 + qt:5 + qt], zz[:, 9:10], nlam[:, 0:1])
                            z_tok = sYs.inc(zt, 1)
                            vv.wait_ge(sYs.h, z_tok)
                            vv.wait_ge(sPVt.h, pv_done)
                            if ytr_prev:
                                vv.wait_ge(sYTt.h, ytr_prev)
                            for qt in range(4):
                                e = vv.scalar_tensor_tensor(ybf[:, qt * 256:(qt + 1) * 256],
                                                            py[:, qt * 512:qt * 512 + 256],
                                                            zz[:, 4 + qt:5 + qt],
                                                            ytmp[:, qt * 256:(qt + 1) * 256],
                                                            op0=ALU.mult, op1=ALU.add)
                                evq[qt] = sYv.inc(e, 1)
                            prev_ev = list(evq)
                    ti = None
                    for qt in range(4):
                        tv.wait_ge(sYv.h, evq[qt])
                        tv.wait_ge(sYTv.h, sYTv.n)
                        for ct in range(2):
                            ti = tv.transpose(pty[:, ct * 128:(ct + 1) * 128],
                                              ybf[:, qt * 256 + ct * 128:qt * 256 + (ct + 1) * 128],
                                              identb[:, :])
                        ytok = sYTt.inc(ti, 1)
                        vv.wait_ge(sYTt.h, ytok)
                        if qt == 0 and yst_tok:
                            vv.wait_ge(sYW.h, yst_tok)
                        c1 = vv.tensor_copy(yst[:, qt * 128:(qt + 1) * 128], pty[:, 0:128])
                        c2 = vv.tensor_copy(yst[:, TCK + qt * 128:TCK + (qt + 1) * 128], pty[:, 128:256])
                        sYTv.inc(c1, 1)
                        sYTv.inc(c2, 1)
                    ytr_prev = sYTt.n
                    yv.wait_ge(sYTv.h, sYTv.n)
                    sYW.inc(yv.dma_start(yT_s[:, h * 2, qsc * TCK:(qsc + 1) * TCK], yst[:, 0:TCK]), 16)
                    sYW.inc(yv.dma_start(yT_s[:, h * 2 + 1, qsc * TCK:(qsc + 1) * TCK], yst[:, TCK:2 * TCK]), 16)
                    yst_tok = sYW.n
        if "att" in debug_outputs:
            qd = [nc.dram_tensor(f"qk_dbg{i}", [128, T], BF16, kind="ExternalOutput").ap() for i in range(4)]
            vd = nc.dram_tensor("vp_dbg", [128, NTT * VS], BF16, kind="ExternalOutput").ap()
            zzd = nc.dram_tensor("zz_dbg", [128, 16], F32, kind="ExternalOutput").ap()
            ytd = nc.dram_tensor("ytmp_dbg", [128, 4 * 256], F32, kind="ExternalOutput").ap()
            ybd = nc.dram_tensor("ybf_dbg", [128, 4 * 256], BF16, kind="ExternalOutput").ap()
            pbd = nc.dram_tensor("pbuf_dbg", [128, 4 * TCK], BF16, kind="ExternalOutput").ap()
            sDbg = Sem(nc, "sDbg")
            yv.wait_ge(sYW.h, sYW.n)
            for i, t_ in enumerate((qnT1, qnT2, knT1, knT2)):
                sDbg.inc(yv.dma_start(qd[i], t_[:, :]), 16)
            sDbg.inc(yv.dma_start(vd, vp[:, :]), 16)
            sDbg.inc(yv.dma_start(zzd, zz[:, :]), 16)
            sDbg.inc(yv.dma_start(ytd, ytmp[:, :]), 16)
            sDbg.inc(yv.dma_start(ybd, ybf[:, :]), 16)
            sDbg.inc(yv.dma_start(pbd, pbuf[:, :]), 16)
            mkd = nc.dram_tensor("masks_dbg", [128, 4 * TCK], BF16, kind="ExternalOutput").ap()
            sDbg.inc(yv.dma_start(mkd, masks[:, :]), 16)
            nld = nc.dram_tensor("nlam_dbg", [128, 1], F32, kind="ExternalOutput").ap()
            sDbg.inc(yv.dma_start(nld, nlam[:, :]), 16)
            yv.wait_ge(sDbg.h, sDbg.n)
        a1_yw = sYW.n

    if stop_after == "a1":
        yv.wait_ge(sYW.h, sYW.n)
        finish_early()
        return nc

    with (
        nc.sbuf_tensor("ytb", [128, 2 * NJT * TCK], BF16) as ytb,
        nc.sbuf_tensor("wob", [128, NJT * D], BF16) as wob,
        nc.sbuf_tensor("xb2", [128, 2 * D], F32) as xb2,
        nc.sbuf_tensor("x2st", [128, D], F32) as x2st,
        nc.sbuf_tensor("sqs2", [128, D], F32) as sqs2,
        nc.sbuf_tensor("ln2b", [128, D], BF16) as ln2b,
        nc.sbuf_tensor("stage2", [128, NJT * TCK], BF16) as stage2,
        nc.sbuf_tensor("rwb", [128, NJT * 8], BF16) as rwb,
        nc.sbuf_tensor("cwT_full", [8, T], F32) as cwT_full,
        nc.sbuf_tensor("rsc", [128, 24], F32) as rsc,
        nc.psum_tensor("pout", [128, D], F32) as pout,
        nc.psum_tensor("pscr", [128, 8], F32) as pscr,
        nc.psum_tensor("pt2", [128, 2 * 1024], BF16) as pt2,
        nc.psum_tensor("pcw", [128, 128], F32) as pcw,
    ):
        if not skip_p0:
            yv.wait_ge(sP0s.h, 16 * p0_attn_jobs)
        sWq.inc(yv.dma_start(wob[:, :], wob_s), 16)
        sWq.inc(yv.dma_start(rwb[:, :], rwb_s), 16)
        wo_need = sWq.n
        pend_cw = [None]

        def emit_cw_transpose():
            if pend_cw[0] is None:
                return
            ptt, ptok = pend_cw[0]
            pend_cw[0] = None
            tv.wait_ge(sA2v.h, ptok)
            if sCWv.n:
                tv.wait_ge(sCWv.h, sCWv.n)
            ct_tok = sCWt.inc(tv.matmul(pcw[0:8, 0:128], cw_sb[:, ptt * 8:(ptt + 1) * 8],
                                        identf[:, :], start=True, stop=True), 1)
            vv.wait_ge(sCWt.h, ct_tok)
            sCWv.inc(vv.tensor_copy(cwT_full[0:8, ptt * 128:(ptt + 1) * 128], pcw[0:8, 0:128]), 1)
        emit_p0(len(p0_jobs))
        assert skip_p0 or p0_state["i"] == len(p0_jobs)

        slab_tok2 = [0, 0]
        x_tok = [0, 0]
        pout_tok = [0, 0, 0, 0]
        st2_tok = [0, 0]
        flush2_tok = 0
        x2w_prev = 0
        ln2_prev = 0
        ln2_tr_prev = 0
        ssq_prev = 0
        for tc in range(NTC):
            sl = tc % 2
            if slab_tok2[sl]:
                yv.wait_ge(sOPt.h, slab_tok2[sl])
            if tc == 0:
                yv.wait_ge(sYW.h, a1_yw)
            for qtr in range(4):
                src = yT_s[:, qtr * 4:(qtr + 1) * 4, tc * TCK:(tc + 1) * TCK]
                dst = ytb[:, sl * NJT * TCK:(sl + 1) * NJT * TCK].rearrange(
                    "p (j c) -> p j c", c=TCK)[:, qtr * 4:(qtr + 1) * 4, :]
                sYR.inc(yv.dma_start(dst, src), 16)
            slab_need = sYR.n
            op_tok = 0
            for tl in range(4):
                tt = tc * 4 + tl
                xs = xb2[:, (tt % 2) * D:(tt % 2 + 1) * D]
                if x_tok[tt % 2]:
                    yv.wait_ge(sA2v.h, x_tok[tt % 2])
                sX.inc(yv.dma_start(xs, x_d[tt * TTK:(tt + 1) * TTK, :]), 16)
                x_need = sX.n
                tv.wait_ge(sYR.h, slab_need)
                tv.wait_ge(sWq.h, wo_need)
                for mc in range(4):
                    if pout_tok[mc]:
                        tv.wait_ge(sA2v.h, pout_tok[mc])
                    mm = None
                    for c in range(NJT):
                        lhsT = ytb[:, sl * NJT * TCK + c * TCK + tl * 128:
                                   sl * NJT * TCK + c * TCK + (tl + 1) * 128]
                        rhs = wob[:, c * D + mc * 512:c * D + (mc + 1) * 512]
                        mm = tv.matmul(pout[:, mc * 512:(mc + 1) * 512], lhsT, rhs,
                                       start=(c == 0), stop=(c == NJT - 1))
                    op_tok = sOPt.inc(mm, 1)
                    vv.wait_ge(sOPt.h, op_tok)
                    vv.wait_ge(sX.h, x_need)
                    if mc == 0:
                        if x2w_prev:
                            vv.wait_ge(sX2W.h, x2w_prev)
                        if ssq_prev:
                            vv.wait_ge(sA2s.h, ssq_prev)
                    ev = vv.scalar_tensor_tensor(x2st[:, mc * 512:(mc + 1) * 512],
                                                 pout[:, mc * 512:(mc + 1) * 512],
                                                 1.0 - LAMBDA_INIT,
                                                 xs[:, mc * 512:(mc + 1) * 512],
                                                 op0=ALU.mult, op1=ALU.add)
                    pout_tok[mc] = sA2v.inc(ev, 1)
                x_tok[tt % 2] = sA2v.n
                x2_ready = sA2v.n
                yv.wait_ge(sA2v.h, x2_ready)
                sX2W.inc(yv.dma_start(x2_s[tt], x2st[:, :]), 16)
                x2w_prev = sX2W.n
                do_ln2 = a2_sub in (None, "ln2", "router")
                do_rt = a2_sub in (None, "router")
                do_cwt = a2_sub is None
                st = stats[:, (tt % 2) * 8:(tt % 2) * 8 + 8]
                if not do_ln2:
                    continue
                sv.wait_ge(sA2v.h, x2_ready)
                if tt >= 1:
                    sv.wait_ge(sA2v.h, sA2v.n)
                ssq_tok = sA2s.inc(sv.activation(sqs2[:, :], x2st[:, :], AF.Square, accum_out=st[:, 0:1]), 1)
                ssq_prev = ssq_tok
                t_r = sA2v.inc(vv.reduce_sum(st[:, 1:2], x2st[:, :], axis=AX.X), 1)
                vv.wait_ge(sA2s.h, ssq_tok)
                vv.wait_ge(sA2v.h, t_r)
                t_m = sA2v.inc(vv.tensor_scalar(st[:, 2:3], st[:, 1:2], 1.0 / D, None, op0=ALU.mult), 1)
                sA2v.inc(vv.tensor_scalar(st[:, 3:4], st[:, 0:1], 1.0 / D, None, op0=ALU.mult), 1)
                vv.wait_ge(sA2v.h, t_m)
                t_n = sA2v.inc(vv.tensor_scalar(st[:, 4:5], st[:, 2:3], st[:, 2:3], -1.0, op0=ALU.mult, op1=ALU.mult), 1)
                vv.wait_ge(sA2v.h, t_n)
                var_tok = sA2v.inc(vv.tensor_add(st[:, 3:4], st[:, 3:4], st[:, 4:5]), 1)
                sv.wait_ge(sA2v.h, var_tok)
                sA2s.inc(sv.activation(st[:, 5:6], st[:, 3:4], AF.Ln, bias=eps_ln[:, 0:1]), 1)
                rstd_tok = sA2s.inc(sv.activation(st[:, 6:7], st[:, 5:6], AF.Exp, scale=-0.5), 1)
                vv.wait_ge(sA2s.h, rstd_tok)
                if tt >= 1:
                    vv.wait_ge(sRt.h, ln2_tr_prev)
                ln2_prev = sA2v.inc(vv.tensor_scalar(ln2b[:, :], x2st[:, :], st[:, 2:3], st[:, 6:7],
                                                     op0=ALU.subtract, op1=ALU.mult), 1)
                tv.wait_ge(sA2v.h, ln2_prev)
                for b in range(2):
                    if st2_tok[b]:
                        tv.wait_ge(sA2v.h, st2_tok[b])
                    ti = None
                    for k in range(8):
                        jt = b * 8 + k
                        ti = tv.transpose(pt2[:, b * 1024 + k * 128:b * 1024 + (k + 1) * 128],
                                          ln2b[:, jt * 128:(jt + 1) * 128], identb[:, :])
                    tr_tok = sRt.inc(ti, 1)
                    vv.wait_ge(sRt.h, tr_tok)
                    if tl == 0 and b == 0 and flush2_tok:
                        vv.wait_ge(sL2W.h, flush2_tok)
                    dst = stage2[:, :].rearrange("p (j q c) -> p j q c", q=4, c=128)[:, b * 8:(b + 1) * 8, tl, :]
                    src = pt2[:, b * 1024:(b + 1) * 1024].rearrange("p (j c) -> p j c", c=128)
                    st2_tok[b] = sA2v.inc(vv.tensor_copy(dst, src), 1)
                ln2_tr_prev = sRt.n
                stage_tok = sA2v.n
                if not do_rt:
                    continue
                tv.wait_ge(sA2v.h, stage_tok)
                mm = None
                for j in range(NJT):
                    lhsT = stage2[:, :].rearrange("p (j q c) -> p j q c", q=4, c=128)[:, j, tl, :]
                    mm = tv.matmul(pscr[:, 0:8], lhsT, rwb[:, j * 8:(j + 1) * 8],
                                   start=(j == 0), stop=(j == NJT - 1))
                sc_tok = sRt.inc(mm, 1)
                sv.wait_ge(sRt.h, sc_tok)
                exp_tok = sA2s.inc(sv.activation(rsc[:, 8:16], pscr[:, 0:8], AF.Exp), 1)
                vv.wait_ge(sA2s.h, exp_tok)
                t_sum = sA2v.inc(vv.reduce_sum(rsc[:, 1:2], rsc[:, 8:16], axis=AX.X), 1)
                sv.wait_ge(sA2v.h, t_sum)
                sv.activation(rsc[:, 2:3], rsc[:, 1:2], AF.Ln)
                t_rs = sA2s.inc(sv.activation(rsc[:, 2:3], rsc[:, 2:3], AF.Exp, scale=-1.0), 1)
                vv.wait_ge(sA2s.h, t_rs)
                t_p = sA2v.inc(vv.tensor_scalar(rsc[:, 8:16], rsc[:, 8:16], rsc[:, 2:3], None, op0=ALU.mult), 1)
                vv.wait_ge(sA2v.h, t_p)
                t_m1 = sA2v.inc(vv.reduce_max(rsc[:, 3:4], rsc[:, 8:16], axis=AX.X), 1)
                vv.wait_ge(sA2v.h, t_m1)
                t_k1 = sA2v.inc(vv.tensor_scalar(rsc[:, 16:24], rsc[:, 8:16], rsc[:, 3:4], None, op0=ALU.is_ge), 1)
                vv.wait_ge(sA2v.h, t_k1)
                t_pm = sA2v.inc(vv.scalar_tensor_tensor(rsc[:, 16:24], rsc[:, 16:24], -1e30, rsc[:, 8:16],
                                                        op0=ALU.mult, op1=ALU.add), 1)
                vv.wait_ge(sA2v.h, t_pm)
                t_m2 = sA2v.inc(vv.reduce_max(rsc[:, 4:5], rsc[:, 16:24], axis=AX.X), 1)
                vv.wait_ge(sA2v.h, t_m2)
                t_dn = sA2v.inc(vv.tensor_add(rsc[:, 5:6], rsc[:, 3:4], rsc[:, 4:5]), 1)
                sv.wait_ge(sA2v.h, t_dn)
                sv.activation(rsc[:, 6:7], rsc[:, 5:6], AF.Ln)
                t_rd = sA2s.inc(sv.activation(rsc[:, 6:7], rsc[:, 6:7], AF.Exp, scale=-1.0), 1)
                t_k2 = sA2v.inc(vv.tensor_scalar(rsc[:, 16:24], rsc[:, 8:16], rsc[:, 4:5], None, op0=ALU.is_ge), 1)
                vv.wait_ge(sA2v.h, t_k2)
                t_pk = sA2v.inc(vv.tensor_mul(rsc[:, 8:16], rsc[:, 8:16], rsc[:, 16:24]), 1)
                vv.wait_ge(sA2v.h, t_pk)
                vv.wait_ge(sA2s.h, t_rd)
                sA2v.inc(vv.tensor_scalar(cw_sb[:, tt * 8:(tt + 1) * 8], rsc[:, 8:16],
                                          rsc[:, 6:7], None, op0=ALU.mult), 1)
                if not do_cwt:
                    continue
                emit_cw_transpose()
                pend_cw[0] = (tt, sA2v.n)
            slab_tok2[sl] = op_tok
            yv.wait_ge(sA2v.h, sA2v.n)
            for half in range(4):
                src = stage2[:, :].rearrange("p (j c) -> p j c", c=TCK)[:, half * 4:(half + 1) * 4, :]
                sL2W.inc(yv.dma_start(ln2T_s[:, half * 4:(half + 1) * 4, tc * TCK:(tc + 1) * TCK], src), 16)
            flush2_tok = sL2W.n
        emit_cw_transpose()
        yv.wait_ge(sCWv.h, sCWv.n)
        sCWw.inc(yv.dma_start(cwT_s[0:8, :], cwT_full[0:8, :]), 16)
        if cw_dbg is not None:
            yv.wait_ge(sA2v.h, sA2v.n)
            sOW.inc(yv.dma_start(cw_dbg, cw_sb[:, :]), 16)
        a2_done = dict(l2w=sL2W.n, x2w=sX2W.n, cww=sCWw.n)

    if stop_after == "a2":
        yv.wait_ge(sL2W.h, sL2W.n)
        yv.wait_ge(sX2W.h, sX2W.n)
        yv.wait_ge(sCWw.h, sCWw.n)
        yv.wait_ge(sP0s.h, (0 if skip_p0 else 16 * len(p0_jobs)))
        finish_early()
        return nc

    if debug_outputs and "out" not in debug_outputs:
        yv.wait_ge(sL1W.h, sL1W.n)
        yv.wait_ge(sYW.h, sYW.n)
        yv.wait_ge(sL2W.h, sL2W.n)
        yv.wait_ge(sX2W.h, sX2W.n)
        yv.wait_ge(sP0s.h, (0 if skip_p0 else 16 * len(p0_jobs)))
        yv.wait_ge(sOW.h, sOW.n)
        return nc

    p0_ctx.close()

    with (
        nc.sbuf_tensor("l2b", [128, NJT * TCK], BF16) as l2b,
        nc.sbuf_tensor("actb", [128, E * NFT * 512], BF16) as actb,
        nc.sbuf_tensor("gupb", [128, 2 * NJT * 256], BF16) as gupb,
        nc.sbuf_tensor("dwnb", [128, 2 * NFT * 1024], BF16) as dwnb,
        nc.sbuf_tensor("cwbb", [128, 2 * TCK], F32) as cwbb,
        nc.sbuf_tensor("cwT8b", [8, 2 * TCK], F32) as cwT8b,
        nc.sbuf_tensor("x2rb", [128, 4 * 1024], F32) as x2rb,
        nc.sbuf_tensor("outst", [128, 4 * 1024], F32) as outst,
        nc.sbuf_tensor("silub", [128, 512], F32) as silub,
        nc.sbuf_tensor("acttmp", [128, 512], F32) as acttmp,
    ):
        gw_tok = [0, 0]
        dw_tok = [0, 0]
        l2_tok = 0
        x2r_tok = 0
        out_tok = 0
        b2t_prev = 0
        sv.wait_ge(sP0s.h, (0 if skip_p0 else 16 * len(p0_jobs)))
        for tc in range(NTC):
            if l2_tok:
                yv.wait_ge(sB1t.h, l2_tok)
            if tc == 0:
                yv.wait_ge(sL2W.h, a2_done["l2w"])
            for qtr in range(4):
                src = ln2T_s[:, qtr * 4:(qtr + 1) * 4, tc * TCK:(tc + 1) * TCK]
                dst = l2b[:, :].rearrange("p (j c) -> p j c", c=TCK)[:, qtr * 4:(qtr + 1) * 4, :]
                sL2R.inc(yv.dma_start(dst, src), 16)
            l2_need = sL2R.n
            if tc == 0:
                yv.wait_ge(sCWw.h, a2_done["cww"])
            if tc >= 2:
                yv.wait_ge(sCBt.h, sCBt.n)
            sCTR.inc(yv.dma_start(cwT8b[0:8, (tc % 2) * TCK:(tc % 2 + 1) * TCK],
                                  cwT_s[0:8, tc * TCK:(tc + 1) * TCK]), 16)
            ct_need = sCTR.n

            with (
                nc.psum_tensor(f"pgu{tc}", [128, 4 * 512], F32) as pgu,
                nc.psum_tensor(f"pcwb{tc}", [128, 512], F32) as pcwb,
            ):
                pgu_s_tok = [0, 0]
                pgu_v_tok = [0, 0]
                silu_v_tok = 0
                for e in range(E):
                    tv.wait_ge(sCTR.h, ct_need)
                    if sCBv.n:
                        tv.wait_ge(sCBv.h, sCBv.n)
                    cb = tv.matmul(pcwb[:, :], sel[0:8, e * 128:(e + 1) * 128],
                                   cwT8b[0:8, (tc % 2) * TCK:(tc % 2 + 1) * TCK], start=True, stop=True)
                    cb_tok = sCBt.inc(cb, 1)
                    vv.wait_ge(sCBt.h, cb_tok)
                    if e >= 2:
                        pass
                    cwb_ready = sCBv.inc(vv.tensor_copy(cwbb[:, (e % 2) * TCK:(e % 2 + 1) * TCK], pcwb[:, :]), 1)
                    for ft in range(NFT):
                        gslot = (e * NFT + ft) % 2
                        if gw_tok[gslot]:
                            sv.wait_ge(sB1t.h, gw_tok[gslot])
                        sGW.inc(sv.dma_start(gupb[:, gslot * NJT * 256:(gslot + 1) * NJT * 256],
                                             gupb_s[e, ft]), 16)
                        g_need = sGW.n
                        par = ft % 2
                        gb, ub = 2 * par, 2 * par + 1
                        if pgu_s_tok[par]:
                            tv.wait_ge(sB1s.h, pgu_s_tok[par])
                        if pgu_v_tok[par]:
                            tv.wait_ge(sB1v.h, pgu_v_tok[par])
                        tv.wait_ge(sGW.h, g_need)
                        tv.wait_ge(sL2R.h, l2_need)
                        mm = None
                        for j in range(NJT):
                            base = gslot * NJT * 256 + j * 256
                            lnt = l2b[:, j * TCK:(j + 1) * TCK]
                            tv.matmul(pgu[:, gb * 512:(gb + 1) * 512],
                                      gupb[:, base:base + 128], lnt,
                                      start=(j == 0), stop=(j == NJT - 1))
                            mm = tv.matmul(pgu[:, ub * 512:(ub + 1) * 512],
                                           gupb[:, base + 128:base + 256], lnt,
                                           start=(j == 0), stop=(j == NJT - 1))
                        mm_tok = sB1t.inc(mm, 1)
                        gw_tok[gslot] = mm_tok
                        sv.wait_ge(sB1t.h, mm_tok)
                        if silu_v_tok:
                            sv.wait_ge(sB1v.h, silu_v_tok)
                        si = sv.activation(silub[:, :], pgu[:, gb * 512:(gb + 1) * 512], AF.Silu)
                        si_tok = sB1s.inc(si, 1)
                        pgu_s_tok[par] = si_tok
                        vv.wait_ge(sB1s.h, si_tok)
                        if b2t_prev:
                            vv.wait_ge(sB2t.h, b2t_prev)
                        t1 = vv.tensor_mul(acttmp[:, :], silub[:, :], pgu[:, ub * 512:(ub + 1) * 512])
                        up_tok = sB1v.inc(t1, 1)
                        silu_v_tok = up_tok
                        pgu_v_tok[par] = up_tok
                        vv.wait_ge(sCBv.h, cwb_ready)
                        vv.wait_ge(sB1v.h, up_tok)
                        t2 = vv.tensor_mul(actb[:, (e * NFT + ft) * 512:(e * NFT + ft + 1) * 512],
                                           acttmp[:, :], cwbb[:, (e % 2) * TCK:(e % 2 + 1) * TCK])
                        sB1v.inc(t2, 1)
                l2_tok = sB1t.n
                b1_v_final = sB1v.n

            with nc.psum_tensor(f"pdn{tc}", [128, 8 * 512], F32) as pdn:
                for dh in range(2):
                    if x2r_tok:
                        yv.wait_ge(sB2v.h, x2r_tok)
                    if tc == 0 and dh == 0:
                        yv.wait_ge(sX2W.h, a2_done["x2w"])
                    src = x2_s[tc * 4:(tc + 1) * 4, :, dh * 1024:(dh + 1) * 1024].rearrange("t p c -> p t c")
                    dst = x2rb[:, :].rearrange("p (t c) -> p t c", c=1024)
                    sX2R.inc(yv.dma_start(dst, src), 16)
                    x2_need = sX2R.n
                    for e in range(E):
                        dslot = e % 2
                        if dw_tok[dslot]:
                            sv.wait_ge(sB2t.h, dw_tok[dslot])
                        sDW.inc(sv.dma_start(dwnb[:, dslot * NFT * 1024:(dslot + 1) * NFT * 1024],
                                             dwnb_s[e, dh]), 16)
                        d_need = sDW.n
                        tv.wait_ge(sDW.h, d_need)
                        if e == 0:
                            tv.wait_ge(sB1v.h, b1_v_final)
                            tv.wait_ge(sB2v.h, sB2v.n)
                        mm = None
                        for f in range(NFT):
                            for tl2 in (3, 2, 1, 0):
                                lhsT = actb[:, (e * NFT + f) * 512 + tl2 * 128:
                                            (e * NFT + f) * 512 + (tl2 + 1) * 128]
                                for dc in range(2):
                                    bank = tl2 * 2 + dc
                                    mm = tv.matmul(pdn[:, bank * 512:(bank + 1) * 512],
                                                   lhsT,
                                                   dwnb[:, dslot * NFT * 1024 + f * 1024 + dc * 512:
                                                        dslot * NFT * 1024 + f * 1024 + (dc + 1) * 512],
                                                   start=(e == 0 and f == 0), stop=(e == E - 1 and f == NFT - 1))
                        dw_tok[dslot] = sB2t.inc(mm, 1)
                    b2t_prev = sB2t.n
                    vv.wait_ge(sB2t.h, b2t_prev)
                    vv.wait_ge(sX2R.h, x2_need)
                    if out_tok:
                        vv.wait_ge(sOW.h, out_tok)
                    for tl2 in range(4):
                        for dc in range(2):
                            bank = tl2 * 2 + dc
                            ee = vv.tensor_add(outst[:, tl2 * 1024 + dc * 512:tl2 * 1024 + (dc + 1) * 512],
                                               pdn[:, bank * 512:(bank + 1) * 512],
                                               x2rb[:, tl2 * 1024 + dc * 512:tl2 * 1024 + (dc + 1) * 512])
                            sB2v.inc(ee, 1)
                    x2r_tok = sB2v.n
                    yv.wait_ge(sB2v.h, sB2v.n)
                    for tl2 in range(4):
                        tt = tc * 4 + tl2
                        sOW.inc(yv.dma_start(out_d[tt * TTK:(tt + 1) * TTK, dh * 1024:(dh + 1) * 1024],
                                             outst[:, tl2 * 1024:(tl2 + 1) * 1024]), 16)
                    out_tok = sOW.n

    yv.wait_ge(sOW.h, sOW.n)
    return nc


def _prep_shared(i):
    f32 = np.float32
    wq = np.asarray(i["L0_attn_wq"], f32)
    wk = np.asarray(i["L0_attn_wk"], f32)
    wv = np.asarray(i["L0_attn_wv"], f32)
    wo = np.asarray(i["L0_attn_wo"], f32)
    rw = np.asarray(i["L0_ffn_router_w"], f32)
    gate = np.asarray(i["L0_ffn_gate_proj"], f32)
    up = np.asarray(i["L0_ffn_up_proj"], f32)
    dwn = np.asarray(i["L0_ffn_down_proj"], f32)

    wqkvT = np.stack([
        np.concatenate([wq[h * 256:(h + 1) * 256].T,
                        wk[h * 256:(h + 1) * 256].T,
                        wv[h * 256:(h + 1) * 256].T], axis=1)
        for h in range(H)
    ])
    wqkvT = np.ascontiguousarray(
        wqkvT.reshape(H, NJT, 128, 768).transpose(0, 2, 1, 3)).reshape(H, 128, NJT * 768)

    woT = np.ascontiguousarray(wo.T.reshape(NJT, 128, D).transpose(1, 0, 2)).reshape(128, NJT * D)
    rwt = np.ascontiguousarray(rw.reshape(NJT, 128, 8).transpose(1, 0, 2)).reshape(128, NJT * 8)

    g = gate.reshape(E, NJT, 128, NFT, 128).transpose(0, 3, 2, 1, 4)
    u = up.reshape(E, NJT, 128, NFT, 128).transpose(0, 3, 2, 1, 4)
    gup = np.ascontiguousarray(np.concatenate([g, u], axis=4)).reshape(E, NFT, 128, NJT * 256)

    dw = dwn.reshape(E, NFT, 128, 2, 1024).transpose(0, 3, 2, 1, 4)
    dw = np.ascontiguousarray(dw).reshape(E, 2, 128, NFT * 1024)

    lamv = np.concatenate([np.asarray(i["L0_attn_lq1"], f32), np.asarray(i["L0_attn_lk1"], f32),
                           np.asarray(i["L0_attn_lq2"], f32), np.asarray(i["L0_attn_lk2"], f32)]).reshape(1, 1024)

    k_idx = np.arange(128)[:, None]
    q_idx = np.arange(TCK)[None, :]
    masksm = np.concatenate([(q_idx >= k_idx + 128 * o).astype(f32)
                             for o in range(4)], axis=1)

    identf = np.eye(128, dtype=f32)
    selm = np.zeros((8, 8 * 128), f32)
    for e in range(E):
        selm[e, e * 128:(e + 1) * 128] = 1.0
    ones32 = np.ones((1, 128), f32)

    return {"wqkvT": wqkvT, "woT": woT, "rw": rwt, "gup": gup, "dwn": dw,
            "lam_d": lamv, "masks_d": masksm,
            "identf_d": identf, "sel_d": selm, "ones32_d": ones32}


def kernel(**inputs):
    x = np.asarray(inputs["L0_attn_x"], np.float32)
    shared = _prep_shared(inputs)
    nc = build_program()
    in_maps = []
    for c in range(8):
        m = dict(shared)
        m["x"] = np.ascontiguousarray(x[c])
        in_maps.append(m)
    res = run_bass_kernel_spmd(nc, in_maps, core_ids=list(range(8)))
    return np.stack([res.results[c]["out"] for c in range(8)])


# version 04
# speedup vs baseline: 1.0172x; 1.0172x over previous
from contextlib import ExitStack

import numpy as np
import ml_dtypes
import concourse.bass as bass
import concourse.mybir as mybir
from concourse.bass_utils import run_bass_kernel_spmd

F32 = mybir.dt.float32
BF16 = mybir.dt.bfloat16
AF = mybir.ActivationFunctionType
ALU = mybir.AluOpType
AX = mybir.AxisListType

T, D, H = 4096, 2048, 8
NTT, TTK = 32, 128
NTC, TCK = 8, 512
NJT = 16
E, FF, NFT = 8, 1024, 8
VS = 264
SCALE = float(1.0 / np.sqrt(128.0))
EXPSHIFT = -11.5
LN_EPS = 1e-5
RMS_EPS = 1.1920929e-07
LAMBDA_INIT = 0.8


class Sem:
    def __init__(self, nc, name):
        self.h = nc.alloc_semaphore(name=name)
        self.n = 0

    def inc(self, instr, k=1):
        instr.then_inc(self.h, k)
        self.n += k
        return self.n


def build_program(debug_outputs=(), stop_after=None, a2_sub=None, skip_p0=False):
    nc = bass.Bass("TRN2", target_bir_lowering=False)

    def dk(name):
        return "ExternalOutput" if name in debug_outputs else "Internal"

    x_d = nc.dram_tensor("x", [T, D], F32, kind="ExternalInput").ap()
    wqkv_d = nc.dram_tensor("wqkvT", [H, 128, NJT * 768], F32, kind="ExternalInput").ap()
    wo_d = nc.dram_tensor("woT", [128, NJT * D], F32, kind="ExternalInput").ap()
    rw_d = nc.dram_tensor("rw", [128, NJT * 8], F32, kind="ExternalInput").ap()
    gup_d = nc.dram_tensor("gup", [E, NFT, 128, NJT * 256], F32, kind="ExternalInput").ap()
    dwn_d = nc.dram_tensor("dwn", [E, 2, 128, NFT * 1024], F32, kind="ExternalInput").ap()
    lam_d = nc.dram_tensor("lam_d", [1, 1024], F32, kind="ExternalInput").ap()
    masks_d = nc.dram_tensor("masks_d", [128, 4 * TCK], F32, kind="ExternalInput").ap()
    identf_d = nc.dram_tensor("identf_d", [128, 128], F32, kind="ExternalInput").ap()
    sel_d = nc.dram_tensor("sel_d", [8, 8 * 128], F32, kind="ExternalInput").ap()
    ones_d = nc.dram_tensor("ones32_d", [1, 128], F32, kind="ExternalInput").ap()

    out_d = nc.dram_tensor("out", [T, D], F32, kind="ExternalOutput").ap()

    ln1T_s = nc.dram_tensor("ln1T_s", [128, NJT, T], BF16, kind=dk("ln1T_s")).ap()
    yT_s = nc.dram_tensor("yT_s", [128, NJT, T], BF16, kind=dk("yT_s")).ap()
    ln2T_s = nc.dram_tensor("ln2T_s", [128, NJT, T], BF16, kind=dk("ln2T_s")).ap()
    x2_s = nc.dram_tensor("x2_s", [NTT, 128, D], F32, kind=dk("x2_s")).ap()
    gupb_s = nc.dram_tensor("gupb_s", [E, NFT, 128, NJT * 256], BF16, kind="Internal").ap()
    cwT_s = nc.dram_tensor("cwT_s", [8, T], F32, kind=dk("cwT_s")).ap()
    wqkvb_s = nc.dram_tensor("wqkvb_s", [H, 128, NJT * 768], BF16, kind=dk("wqkvb_s")).ap()
    wob_s = nc.dram_tensor("wob_s", [128, NJT * D], BF16, kind="Internal").ap()
    rwb_s = nc.dram_tensor("rwb_s", [128, NJT * 8], BF16, kind="Internal").ap()
    dwnb_s = nc.dram_tensor("dwnb_s", [E, 2, 128, NFT * 1024], BF16, kind="Internal").ap()
    cw_dbg = None
    if "cw" in debug_outputs:
        cw_dbg = nc.dram_tensor("cw", [128, NTT * 8], F32, kind="ExternalOutput").ap()

    identb = nc.alloc_sbuf_tensor("identb", [128, 128], BF16)
    identf = nc.alloc_sbuf_tensor("identf", [128, 128], F32)
    masks = nc.alloc_sbuf_tensor("masks", [128, 4 * TCK], BF16)
    sel = nc.alloc_sbuf_tensor("sel", [8, 8 * 128], F32)
    ones32 = nc.alloc_sbuf_tensor("ones32", [1, 128], F32)
    lam = nc.alloc_sbuf_tensor("lam", [1, 1024], F32)
    lprod = nc.alloc_sbuf_tensor("lprod", [1, 512], F32)
    lsc = nc.alloc_sbuf_tensor("lsc", [1, 8], F32)
    nlam = nc.alloc_sbuf_tensor("nlam", [128, 1], F32)
    bias_exp = nc.alloc_sbuf_tensor("bias_exp", [128, 1], F32)
    eps_ln = nc.alloc_sbuf_tensor("eps_ln", [128, 1], F32)
    eps_rms = nc.alloc_sbuf_tensor("eps_rms", [128, 1], F32)
    cw_sb = nc.alloc_sbuf_tensor("cw_sb", [128, NTT * 8], F32)
    stats = nc.alloc_sbuf_tensor("stats", [128, 16], F32)
    rmsst = nc.alloc_sbuf_tensor("rmsst", [128, 16], F32)

    sems = {}
    def S(name):
        if name not in sems:
            sems[name] = Sem(nc, name)
        return sems[name]

    (sCn, sP0l, sP0s, sX, sA0s, sA0v, sA0t, sL1W, sL1R, sWq, sQKt, sQKs, sQKv,
     sTt, sTv, sSTt, sSTs, sMv, sPVt, sYv, sYTt, sYTv, sYW, sYR, sOPt, sA2v,
     sA2s, sRt, sCWt, sCWv, sL2W, sX2W, sL2R, sX2R, sGW, sDW, sCBt, sCBv,
     sB1t, sB1s, sB1v, sB2t, sB2v, sOW, sLamv, sLams, sLamt, sCWw, sCTR, sYs, sGW2, sDW2) = [S(n) for n in (
        "sCn sP0l sP0s sX sA0s sA0v sA0t sL1W sL1R sWq sQKt sQKs sQKv sTt sTv "
        "sSTt sSTs sMv sPVt sYv sYTt sYTv sYW sYR sOPt sA2v sA2s sRt sCWt sCWv "
        "sL2W sX2W sL2R sX2R sGW sDW sCBt sCBv sB1t sB1s sB1v sB2t sB2v sOW "
        "sLamv sLams sLamt sCWw sCTR sYs sGW2 sDW2").split()]

    tv, vv, sv, gv, yv = nc.tensor, nc.vector, nc.scalar, nc.gpsimd, nc.sync

    sCn.inc(yv.dma_start(identf[:, :], identf_d), 16)
    sCn.inc(yv.dma_start(sel[:, :], sel_d), 16)
    sCn.inc(yv.dma_start(ones32[:, :], ones_d), 16)
    sCn.inc(yv.dma_start(lam[:, :], lam_d), 16)
    sCn.inc(gv.memset(bias_exp[:, :], EXPSHIFT), 1)
    sCn.inc(gv.memset(eps_ln[:, :], LN_EPS), 1)
    sCn.inc(gv.memset(eps_rms[:, :], RMS_EPS), 1)
    sCn.inc(gv.memset(stats[:, :], 0.0), 1)
    sCn.inc(gv.memset(rmsst[:, :], 0.0), 1)
    with nc.sbuf_tensor("mstage", [128, 4 * TCK], F32) as mstage:
        ml_tok = Sem(nc, "sMst")
        ml_tok.inc(yv.dma_start(mstage[:, :], masks_d), 16)
        vv.wait_ge(ml_tok.h, ml_tok.n)
        sCn.inc(vv.tensor_copy(masks[:, :], mstage[:, :]), 1)
        vv.wait_ge(sCn.h, 16)
        sCn.inc(vv.tensor_copy(identb[:, :], identf[:, :]), 1)
    cn_target = sCn.n

    with nc.psum_tensor("plam", [128, 1], F32) as plam:
        vv.wait_ge(sCn.h, cn_target)
        t1 = sLamv.inc(vv.tensor_mul(lprod[0:1, 0:256], lam[0:1, 0:256], lam[0:1, 256:512]), 1)
        t2 = sLamv.inc(vv.tensor_mul(lprod[0:1, 256:512], lam[0:1, 512:768], lam[0:1, 768:1024]), 1)
        vv.wait_ge(sLamv.h, t2)
        sLamv.inc(vv.reduce_sum(lsc[0:1, 0:1], lprod[0:1, 0:256], axis=AX.X), 1)
        sLamv.inc(vv.reduce_sum(lsc[0:1, 1:2], lprod[0:1, 256:512], axis=AX.X), 1)
        sv.wait_ge(sLamv.h, sLamv.n)
        sLams.inc(sv.activation(lsc[0:1, 4:6], lsc[0:1, 0:2], AF.Exp), 1)
        vv.wait_ge(sLams.h, sLams.n)
        t3 = sLamv.inc(vv.tensor_sub(lsc[0:1, 6:7], lsc[0:1, 5:6], lsc[0:1, 4:5]), 1)
        vv.wait_ge(sLamv.h, t3)
        sLamv.inc(vv.tensor_scalar(lsc[0:1, 6:7], lsc[0:1, 6:7], -LAMBDA_INIT, None, op0=ALU.add), 1)
        tv.wait_ge(sLamv.h, sLamv.n)
        sLamt.inc(tv.matmul(plam[:, :], ones32[0:1, :], lsc[0:1, 6:7], start=True, stop=True), 1)
        vv.wait_ge(sLamt.h, sLamt.n)
        sLamv.inc(vv.tensor_copy(nlam[:, :], plam[:, :]), 1)
    lam_v_final = sLamv.n

    JW = 2048
    p0_jobs = []
    for h in range(H):
        for c in range(6):
            p0_jobs.append((wqkv_d[h][:, c * JW:(c + 1) * JW], wqkvb_s[h][:, c * JW:(c + 1) * JW]))
    p0_wq_job = [6 * (h + 1) for h in range(H)]
    for c in range(16):
        p0_jobs.append((wo_d[:, c * JW:(c + 1) * JW], wob_s[:, c * JW:(c + 1) * JW]))
    p0_jobs.append((rw_d, rwb_s))
    p0_attn_jobs = len(p0_jobs)
    for e in range(E):
        for ft in range(NFT):
            for c in range(2):
                p0_jobs.append((gup_d[e, ft][:, c * JW:(c + 1) * JW],
                                gupb_s[e, ft][:, c * JW:(c + 1) * JW]))
    for e in range(E):
        for dh in range(2):
            for c in range(4):
                p0_jobs.append((dwn_d[e, dh][:, c * JW:(c + 1) * JW],
                                dwnb_s[e, dh][:, c * JW:(c + 1) * JW]))
    p0_ctx = ExitStack()
    p0_f32 = p0_ctx.enter_context(nc.sbuf_tensor("p0f", [128, 2 * JW], F32))
    p0_bf = p0_ctx.enter_context(nc.sbuf_tensor("p0b", [128, 2 * JW], BF16))
    sP0c = Sem(nc, "sP0c")
    p0_state = {"i": 0}

    def emit_p0(k):
        if skip_p0:
            return
        for _ in range(k):
            i = p0_state["i"]
            if i >= len(p0_jobs):
                return
            srcap, dstap = p0_jobs[i]
            w = srcap.shape[-1]
            sl_ = i % 2
            if i == 0:
                yv.wait_ge(sCn.h, cn_target)
            if i >= 2:
                yv.wait_ge(sP0c.h, i - 1)
            sP0l.inc(yv.dma_start(p0_f32[:, sl_ * JW:sl_ * JW + w], srcap), 16)
            sv.wait_ge(sP0l.h, sP0l.n)
            if i >= 2:
                sv.wait_ge(sP0s.h, 16 * (i - 1))
            sP0c.inc(sv.copy(p0_bf[:, sl_ * JW:sl_ * JW + w], p0_f32[:, sl_ * JW:sl_ * JW + w]), 1)
            yv.wait_ge(sP0c.h, sP0c.n)
            sP0s.inc(yv.dma_start(dstap, p0_bf[:, sl_ * JW:sl_ * JW + w]), 16)
            p0_state["i"] += 1

    def finish_early():
        vv.wait_ge(sCn.h, cn_target)
        dummy = sOW
        fi = yv.dma_start(out_d[0:128, 0:128], identf[:, :])
        dummy.inc(fi, 16)
        yv.wait_ge(dummy.h, dummy.n)

    if stop_after == "init":
        emit_p0(len(p0_jobs))
        gv.wait_ge(sP0s.h, (0 if skip_p0 else 16 * len(p0_jobs)))
        finish_early()
        return nc

    with (
        nc.sbuf_tensor("xb", [128, 2 * D], F32) as xb,
        nc.sbuf_tensor("sqs", [128, D], F32) as sqs,
        nc.sbuf_tensor("ln1b", [128, D], BF16) as ln1b,
        nc.sbuf_tensor("stage0", [128, NJT * TCK], BF16) as stage0,
        nc.psum_tensor("pt0", [128, 2 * 1024], BF16) as pt0,
    ):
        emit_p0(40)
        sv.wait_ge(sCn.h, cn_target)
        vv.wait_ge(sCn.h, cn_target)
        tv.wait_ge(sCn.h, cn_target)
        norm_tok = [0] * NTT
        tr_toks = [0] * NTT
        evac_tok = [0, 0]
        flush_tok = 0
        for tt in range(NTT):
            slx = tt % 2
            if tt >= 2:
                yv.wait_ge(sA0v.h, norm_tok[tt - 2])
            sX.inc(yv.dma_start(xb[:, slx * D:(slx + 1) * D], x_d[tt * TTK:(tt + 1) * TTK, :]), 16)
            x_need = sX.n
            xs = xb[:, slx * D:(slx + 1) * D]
            st = stats[:, (tt % 2) * 8:(tt % 2) * 8 + 8]
            sv.wait_ge(sX.h, x_need)
            if tt >= 1:
                sv.wait_ge(sA0v.h, sA0v.n)
            ssq_tok = sA0s.inc(sv.activation(sqs[:, :], xs, AF.Square, accum_out=st[:, 0:1]), 1)
            vv.wait_ge(sX.h, x_need)
            t_r = sA0v.inc(vv.reduce_sum(st[:, 1:2], xs, axis=AX.X), 1)
            vv.wait_ge(sA0s.h, ssq_tok)
            vv.wait_ge(sA0v.h, t_r)
            t_m = sA0v.inc(vv.tensor_scalar(st[:, 2:3], st[:, 1:2], 1.0 / D, None, op0=ALU.mult), 1)
            sA0v.inc(vv.tensor_scalar(st[:, 3:4], st[:, 0:1], 1.0 / D, None, op0=ALU.mult), 1)
            vv.wait_ge(sA0v.h, t_m)
            t_n = sA0v.inc(vv.tensor_scalar(st[:, 4:5], st[:, 2:3], st[:, 2:3], -1.0, op0=ALU.mult, op1=ALU.mult), 1)
            vv.wait_ge(sA0v.h, t_n)
            var_tok = sA0v.inc(vv.tensor_add(st[:, 3:4], st[:, 3:4], st[:, 4:5]), 1)
            sv.wait_ge(sA0v.h, var_tok)
            sA0s.inc(sv.activation(st[:, 5:6], st[:, 3:4], AF.Ln, bias=eps_ln[:, 0:1]), 1)
            rstd_tok = sA0s.inc(sv.activation(st[:, 6:7], st[:, 5:6], AF.Exp, scale=-0.5), 1)
            vv.wait_ge(sA0s.h, rstd_tok)
            if tt >= 1:
                vv.wait_ge(sA0t.h, tr_toks[tt - 1])
            norm_tok[tt] = sA0v.inc(vv.tensor_scalar(ln1b[:, :], xs, st[:, 2:3], st[:, 6:7],
                                                     op0=ALU.subtract, op1=ALU.mult), 1)
            tv.wait_ge(sA0v.h, norm_tok[tt])
            for b in range(2):
                if evac_tok[b]:
                    tv.wait_ge(sA0v.h, evac_tok[b])
                ti = None
                for k in range(8):
                    jt = b * 8 + k
                    ti = tv.transpose(pt0[:, b * 1024 + k * 128:b * 1024 + (k + 1) * 128],
                                      ln1b[:, jt * 128:(jt + 1) * 128], identb[:, :])
                tr_toks[tt] = sA0t.inc(ti, 1)
                vv.wait_ge(sA0t.h, tr_toks[tt])
                if tt % 4 == 0 and b == 0 and flush_tok:
                    vv.wait_ge(sL1W.h, flush_tok)
                dst = stage0[:, :].rearrange("p (j q c) -> p j q c", q=4, c=128)[:, b * 8:(b + 1) * 8, tt % 4, :]
                src = pt0[:, b * 1024:(b + 1) * 1024].rearrange("p (j c) -> p j c", c=128)
                evac_tok[b] = sA0v.inc(vv.tensor_copy(dst, src), 1)
            if tt % 4 == 3:
                tc = tt // 4
                yv.wait_ge(sA0v.h, sA0v.n)
                for half in range(4):
                    src = stage0[:, :].rearrange("p (j c) -> p j c", c=TCK)[:, half * 4:(half + 1) * 4, :]
                    sL1W.inc(yv.dma_start(ln1T_s[:, half * 4:(half + 1) * 4, tc * TCK:(tc + 1) * TCK], src), 16)
                flush_tok = sL1W.n

    if stop_after == "a0":
        yv.wait_ge(sL1W.h, sL1W.n)
        finish_early()
        return nc

    with (
        nc.sbuf_tensor("l1b", [128, 2 * NJT * TCK], BF16) as l1b,
        nc.sbuf_tensor("wqb", [128, 2 * NJT * 768], BF16) as wqb,
        nc.sbuf_tensor("qnT1", [128, T], BF16) as qnT1,
        nc.sbuf_tensor("qnT2", [128, T], BF16) as qnT2,
        nc.sbuf_tensor("knT1", [128, T], BF16) as knT1,
        nc.sbuf_tensor("knT2", [128, T], BF16) as knT2,
        nc.sbuf_tensor("vp", [128, NTT * VS], BF16) as vp,
        nc.sbuf_tensor("pbuf", [128, 4 * TCK], BF16) as pbuf,
        nc.sbuf_tensor("qkst", [128, TCK], BF16) as qkst,
        nc.sbuf_tensor("sqs1", [128, TCK], F32) as sqs1,
        nc.sbuf_tensor("ytmp", [128, 4 * 256], F32) as ytmp,
        nc.sbuf_tensor("ybf", [128, 4 * 256], BF16) as ybf,
        nc.sbuf_tensor("yst", [128, 2 * TCK], BF16) as yst,
        nc.sbuf_tensor("zz", [128, 16], F32) as zz,
    ):
        vones = vp[:, :].rearrange("p (k c) -> p k c", c=VS)[:, :, 256:257]
        vv.wait_ge(sCn.h, cn_target)
        sCn.inc(vv.memset(vones, 1.0), 1)
        vones_tok = sCn.n

        slab_tok = [0, 0]
        wq_tok = [0, 0]
        if not skip_p0:
            yv.wait_ge(sP0s.h, 16 * p0_wq_job[0])
        sWq.inc(yv.dma_start(wqb[:, 0:NJT * 768], wqkvb_s[0]), 16)
        l1_iter = 0
        prev_ev = [0, 0, 0, 0]
        pb_tok = [0, 0, 0, 0]
        pst_tok = [0, 0, 0]
        yst_tok = 0
        ytr_prev = 0

        for h in range(H):
            with (
                nc.psum_tensor(f"pq{h}", [128, 2 * 1024], F32) as pq,
                nc.psum_tensor(f"ptq{h}", [128, 512], BF16) as ptq,
            ):
                if h + 1 < H:
                    if wq_tok[(h + 1) % 2]:
                        yv.wait_ge(sQKt.h, wq_tok[(h + 1) % 2])
                    if not skip_p0:
                        yv.wait_ge(sP0s.h, 16 * p0_wq_job[h + 1])
                    sWq.inc(yv.dma_start(wqb[:, ((h + 1) % 2) * NJT * 768:((h + 1) % 2 + 1) * NJT * 768],
                                         wqkvb_s[h + 1]), 16)
                wq_need = 16 * (h + 1)
                emit_p0(28)
                if h >= 1:
                    tv.wait_ge(sSTs.h, sSTs.n)
                    tv.wait_ge(sYv.h, sYv.n)
                    tv.wait_ge(sYTv.h, sYTv.n)
                    vv.wait_ge(sSTt.h, sSTt.n)
                    vv.wait_ge(sPVt.h, sPVt.n)
                pq_s_tok = [0, 0]
                pq_v_tok = [0, 0]
                ptq_tok = 0
                qk_ev_tok = [0] * NTT
                for tc in range(NTC):
                    sl = l1_iter % 2
                    if l1_iter >= 2:
                        yv.wait_ge(sQKt.h, slab_tok[sl])
                    if l1_iter == 0:
                        yv.wait_ge(sL1W.h, sL1W.n)
                    for qtr in range(4):
                        src = ln1T_s[:, qtr * 4:(qtr + 1) * 4, tc * TCK:(tc + 1) * TCK]
                        dst = l1b[:, sl * NJT * TCK:(sl + 1) * NJT * TCK].rearrange(
                            "p (j c) -> p j c", c=TCK)[:, qtr * 4:(qtr + 1) * 4, :]
                        sL1R.inc(yv.dma_start(dst, src), 16)
                    slab_need = sL1R.n
                    l1_iter += 1
                    qk_tok = 0
                    for tl in range(4):
                        tt = tc * 4 + tl
                        ps = tt % 2
                        tv.wait_ge(sL1R.h, slab_need)
                        tv.wait_ge(sWq.h, wq_need)
                        if pq_s_tok[ps]:
                            tv.wait_ge(sQKs.h, pq_s_tok[ps])
                        if pq_v_tok[ps]:
                            tv.wait_ge(sQKv.h, pq_v_tok[ps])
                        mm = None
                        for j in range(NJT):
                            lhsT = l1b[:, sl * NJT * TCK + j * TCK + tl * 128:
                                       sl * NJT * TCK + j * TCK + (tl + 1) * 128]
                            rhs = wqb[:, (h % 2) * NJT * 768 + j * 768:(h % 2) * NJT * 768 + (j + 1) * 768]
                            st0 = (j == 0)
                            tv.matmul(pq[:, ps * 1024:ps * 1024 + 512], lhsT, rhs[:, 0:512],
                                      start=st0, stop=(j == NJT - 1))
                            mm = tv.matmul(pq[:, ps * 1024 + 512:ps * 1024 + 768], lhsT, rhs[:, 512:768],
                                           start=st0, stop=(j == NJT - 1))
                        qk_tok = sQKt.inc(mm, 1)
                        sv.wait_ge(sQKt.h, qk_tok)
                        if tt >= 1:
                            sv.wait_ge(sQKv.h, qk_ev_tok[tt - 1])
                        ai = None
                        for g in range(4):
                            ai = sv.activation(sqs1[:, g * 128:(g + 1) * 128],
                                               pq[:, ps * 1024 + g * 128:ps * 1024 + (g + 1) * 128],
                                               AF.Square, accum_out=rmsst[:, g:g + 1])
                        pq_s_tok[ps] = sA = sQKs.inc(ai, 1)
                        rcol = 8 + (tt % 2) * 4
                        sQKs.inc(sv.activation(rmsst[:, 4:8], rmsst[:, 0:4], AF.Ln,
                                               bias=eps_rms[:, 0:1], scale=1.0 / 128.0), 1)
                        rstd_tok = sQKs.inc(sv.activation(rmsst[:, rcol:rcol + 4], rmsst[:, 4:8], AF.Exp, scale=-0.5), 1)
                        vv.wait_ge(sQKs.h, rstd_tok)
                        if ptq_tok:
                            vv.wait_ge(sTt.h, sTt.n)
                        ev = None
                        for g in range(4):
                            ev = vv.tensor_scalar(qkst[:, g * 128:(g + 1) * 128],
                                                  pq[:, ps * 1024 + g * 128:ps * 1024 + (g + 1) * 128],
                                                  rmsst[:, rcol + g:rcol + g + 1], None, op0=ALU.mult)
                        qn_tok = sQKv.inc(ev, 1)
                        vv1 = vv.tensor_copy(vp[:, tt * VS:tt * VS + 256], pq[:, ps * 1024 + 512:ps * 1024 + 768])
                        pq_v_tok[ps] = sQKv.inc(vv1, 1)
                        tv.wait_ge(sQKv.h, qn_tok)
                        if ptq_tok:
                            tv.wait_ge(sQKv.h, ptq_tok)
                        ti = None
                        for g in range(4):
                            ti = tv.transpose(ptq[:, g * 128:(g + 1) * 128],
                                              qkst[:, g * 128:(g + 1) * 128], identb[:, :])
                        tr_tok = sTt.inc(ti, 1)
                        vv.wait_ge(sTt.h, tr_tok)
                        e4 = None
                        for gi, dstT in enumerate((qnT1, qnT2, knT1, knT2)):
                            e4 = vv.tensor_copy(dstT[:, tt * 128:(tt + 1) * 128],
                                                ptq[:, gi * 128:(gi + 1) * 128])
                        ptq_tok = sQKv.inc(e4, 1)
                        qk_ev_tok[tt] = ptq_tok
                    slab_tok[sl] = qk_tok
                if h + 1 < H:
                    wq_tok[(h + 1) % 2] = sQKt.n
                wq_tok[h % 2] = sQKt.n
                qkv_v_final = sQKv.n
                qkv_s_final = sQKs.n

            with (
                nc.psum_tensor(f"pst{h}", [128, 3 * TCK], F32) as pst,
                nc.psum_tensor(f"py{h}", [128, 4 * 512], F32) as py,
                nc.psum_tensor(f"pty{h}", [128, 256], BF16) as pty,
            ):
                if h == 0:
                    tv.wait_ge(sCn.h, vones_tok)
                kctr = 0
                for qsc in range(NTC):
                    nkt = 4 * qsc + 4
                    evq = [0, 0, 0, 0]
                    for var in range(2):
                        qn = qnT1 if var == 0 else qnT2
                        kn = knT1 if var == 0 else knT2
                        tv.wait_ge(sQKv.h, qkv_v_final)
                        tv.wait_ge(sQKs.h, qkv_s_final)

                        pend = []

                        def flush_pv(item, first_base, last_kt):
                            kt, pslot, ex_tok, mask_tok = item
                            if mask_tok:
                                tv.wait_ge(sMv.h, mask_tok)
                            else:
                                tv.wait_ge(sSTs.h, ex_tok)
                            mm = None
                            for qt in range(4):
                                if kt == 0 and first_base[qt]:
                                    tv.wait_ge(sYv.h, first_base[qt])
                                mm = tv.matmul(py[:, qt * 512:qt * 512 + 257],
                                               pbuf[:, pslot * TCK + qt * 128:pslot * TCK + (qt + 1) * 128],
                                               vp[:, kt * VS:kt * VS + 257],
                                               start=(kt == 0), stop=(kt == last_kt))
                            pb_tok[pslot] = sPVt.inc(mm, 1)

                        for kt in range(nkt):
                            b = kctr % 3
                            if pst_tok[b]:
                                tv.wait_ge(sSTs.h, pst_tok[b])
                            mm = tv.matmul(pst[:, b * TCK:(b + 1) * TCK],
                                           kn[:, kt * 128:(kt + 1) * 128],
                                           qn[:, qsc * TCK:(qsc + 1) * TCK],
                                           start=True, stop=True)
                            st_tok = sSTt.inc(mm, 1)
                            pslot = kctr % 4
                            sv.wait_ge(sSTt.h, st_tok)
                            if pb_tok[pslot]:
                                sv.wait_ge(sPVt.h, pb_tok[pslot])
                            ex = sv.activation(pbuf[:, pslot * TCK:(pslot + 1) * TCK],
                                               pst[:, b * TCK:(b + 1) * TCK],
                                               AF.Exp, bias=bias_exp[:, 0:1], scale=SCALE)
                            ex_tok = sSTs.inc(ex, 1)
                            pst_tok[b] = ex_tok
                            o = kt - 4 * qsc
                            mask_tok = None
                            if o >= 0:
                                vv.wait_ge(sSTs.h, ex_tok)
                                if pb_tok[pslot]:
                                    vv.wait_ge(sPVt.h, pb_tok[pslot])
                                mk = vv.tensor_mul(pbuf[:, pslot * TCK:(pslot + 1) * TCK],
                                                   pbuf[:, pslot * TCK:(pslot + 1) * TCK],
                                                   masks[:, o * TCK:(o + 1) * TCK])
                                mask_tok = sMv.inc(mk, 1)
                            pend.append((kt, pslot, ex_tok, mask_tok))
                            kctr += 1
                            if len(pend) > 3:
                                flush_pv(pend.pop(0), prev_ev if var == 0 else evq, nkt - 1)
                        while pend:
                            flush_pv(pend.pop(0), prev_ev if var == 0 else evq, nkt - 1)

                        pv_done = sPVt.n
                        sv.wait_ge(sPVt.h, pv_done)
                        if var == 0:
                            zt = None
                            for qt in range(4):
                                sv.activation(zz[:, 8:9], py[:, qt * 512 + 256:qt * 512 + 257], AF.Ln)
                                zt = sv.activation(zz[:, qt:qt + 1], zz[:, 8:9], AF.Exp, scale=-1.0)
                            z_tok = sYs.inc(zt, 1)
                            vv.wait_ge(sYs.h, z_tok)
                            vv.wait_ge(sPVt.h, pv_done)
                            for qt in range(4):
                                e = vv.tensor_scalar(ytmp[:, qt * 256:(qt + 1) * 256],
                                                     py[:, qt * 512:qt * 512 + 256],
                                                     zz[:, qt:qt + 1], None, op0=ALU.mult)
                                evq[qt] = sYv.inc(e, 1)
                        else:
                            zt = None
                            for qt in range(4):
                                sv.activation(zz[:, 8:9], py[:, qt * 512 + 256:qt * 512 + 257], AF.Ln)
                                sv.activation(zz[:, 9:10], zz[:, 8:9], AF.Exp, scale=-1.0)
                                zt = sv.mul(zz[:, 4 + qt:5 + qt], zz[:, 9:10], nlam[:, 0:1])
                            z_tok = sYs.inc(zt, 1)
                            vv.wait_ge(sYs.h, z_tok)
                            vv.wait_ge(sPVt.h, pv_done)
                            if ytr_prev:
                                vv.wait_ge(sYTt.h, ytr_prev)
                            for qt in range(4):
                                e = vv.scalar_tensor_tensor(ybf[:, qt * 256:(qt + 1) * 256],
                                                            py[:, qt * 512:qt * 512 + 256],
                                                            zz[:, 4 + qt:5 + qt],
                                                            ytmp[:, qt * 256:(qt + 1) * 256],
                                                            op0=ALU.mult, op1=ALU.add)
                                evq[qt] = sYv.inc(e, 1)
                            prev_ev = list(evq)
                    ti = None
                    for qt in range(4):
                        tv.wait_ge(sYv.h, evq[qt])
                        tv.wait_ge(sYTv.h, sYTv.n)
                        for ct in range(2):
                            ti = tv.transpose(pty[:, ct * 128:(ct + 1) * 128],
                                              ybf[:, qt * 256 + ct * 128:qt * 256 + (ct + 1) * 128],
                                              identb[:, :])
                        ytok = sYTt.inc(ti, 1)
                        vv.wait_ge(sYTt.h, ytok)
                        if qt == 0 and yst_tok:
                            vv.wait_ge(sYW.h, yst_tok)
                        c1 = vv.tensor_copy(yst[:, qt * 128:(qt + 1) * 128], pty[:, 0:128])
                        c2 = vv.tensor_copy(yst[:, TCK + qt * 128:TCK + (qt + 1) * 128], pty[:, 128:256])
                        sYTv.inc(c1, 1)
                        sYTv.inc(c2, 1)
                    ytr_prev = sYTt.n
                    yv.wait_ge(sYTv.h, sYTv.n)
                    sYW.inc(yv.dma_start(yT_s[:, h * 2, qsc * TCK:(qsc + 1) * TCK], yst[:, 0:TCK]), 16)
                    sYW.inc(yv.dma_start(yT_s[:, h * 2 + 1, qsc * TCK:(qsc + 1) * TCK], yst[:, TCK:2 * TCK]), 16)
                    yst_tok = sYW.n
        if "att" in debug_outputs:
            qd = [nc.dram_tensor(f"qk_dbg{i}", [128, T], BF16, kind="ExternalOutput").ap() for i in range(4)]
            vd = nc.dram_tensor("vp_dbg", [128, NTT * VS], BF16, kind="ExternalOutput").ap()
            zzd = nc.dram_tensor("zz_dbg", [128, 16], F32, kind="ExternalOutput").ap()
            ytd = nc.dram_tensor("ytmp_dbg", [128, 4 * 256], F32, kind="ExternalOutput").ap()
            ybd = nc.dram_tensor("ybf_dbg", [128, 4 * 256], BF16, kind="ExternalOutput").ap()
            pbd = nc.dram_tensor("pbuf_dbg", [128, 4 * TCK], BF16, kind="ExternalOutput").ap()
            sDbg = Sem(nc, "sDbg")
            yv.wait_ge(sYW.h, sYW.n)
            for i, t_ in enumerate((qnT1, qnT2, knT1, knT2)):
                sDbg.inc(yv.dma_start(qd[i], t_[:, :]), 16)
            sDbg.inc(yv.dma_start(vd, vp[:, :]), 16)
            sDbg.inc(yv.dma_start(zzd, zz[:, :]), 16)
            sDbg.inc(yv.dma_start(ytd, ytmp[:, :]), 16)
            sDbg.inc(yv.dma_start(ybd, ybf[:, :]), 16)
            sDbg.inc(yv.dma_start(pbd, pbuf[:, :]), 16)
            mkd = nc.dram_tensor("masks_dbg", [128, 4 * TCK], BF16, kind="ExternalOutput").ap()
            sDbg.inc(yv.dma_start(mkd, masks[:, :]), 16)
            nld = nc.dram_tensor("nlam_dbg", [128, 1], F32, kind="ExternalOutput").ap()
            sDbg.inc(yv.dma_start(nld, nlam[:, :]), 16)
            yv.wait_ge(sDbg.h, sDbg.n)
        a1_yw = sYW.n

    if stop_after == "a1":
        yv.wait_ge(sYW.h, sYW.n)
        finish_early()
        return nc

    with (
        nc.sbuf_tensor("ytb", [128, 2 * NJT * TCK], BF16) as ytb,
        nc.sbuf_tensor("wob", [128, NJT * D], BF16) as wob,
        nc.sbuf_tensor("xb2", [128, 2 * D], F32) as xb2,
        nc.sbuf_tensor("x2st", [128, D], F32) as x2st,
        nc.sbuf_tensor("sqs2", [128, D], F32) as sqs2,
        nc.sbuf_tensor("ln2b", [128, D], BF16) as ln2b,
        nc.sbuf_tensor("stage2", [128, NJT * TCK], BF16) as stage2,
        nc.sbuf_tensor("rwb", [128, NJT * 8], BF16) as rwb,
        nc.sbuf_tensor("cwT_full", [8, T], F32) as cwT_full,
        nc.sbuf_tensor("rsc", [128, 24], F32) as rsc,
        nc.psum_tensor("pout", [128, D], F32) as pout,
        nc.psum_tensor("pscr", [128, 8], F32) as pscr,
        nc.psum_tensor("pt2", [128, 2 * 1024], BF16) as pt2,
        nc.psum_tensor("pcw", [128, 128], F32) as pcw,
    ):
        if not skip_p0:
            yv.wait_ge(sP0s.h, 16 * p0_attn_jobs)
        sWq.inc(yv.dma_start(wob[:, :], wob_s), 16)
        sWq.inc(yv.dma_start(rwb[:, :], rwb_s), 16)
        wo_need = sWq.n
        pend_cw = [None]

        def emit_cw_transpose():
            if pend_cw[0] is None:
                return
            ptt, ptok = pend_cw[0]
            pend_cw[0] = None
            tv.wait_ge(sA2v.h, ptok)
            if sCWv.n:
                tv.wait_ge(sCWv.h, sCWv.n)
            ct_tok = sCWt.inc(tv.matmul(pcw[0:8, 0:128], cw_sb[:, ptt * 8:(ptt + 1) * 8],
                                        identf[:, :], start=True, stop=True), 1)
            vv.wait_ge(sCWt.h, ct_tok)
            sCWv.inc(vv.tensor_copy(cwT_full[0:8, ptt * 128:(ptt + 1) * 128], pcw[0:8, 0:128]), 1)
        emit_p0(len(p0_jobs))
        assert skip_p0 or p0_state["i"] == len(p0_jobs)

        slab_tok2 = [0, 0]
        x_tok = [0, 0]
        pout_tok = [0, 0, 0, 0]
        st2_tok = [0, 0]
        flush2_tok = 0
        x2w_prev = 0
        ln2_prev = 0
        ln2_tr_prev = 0
        ssq_prev = 0
        ln2_toks = [0] * NTT
        for tc in range(NTC):
            sl = tc % 2
            if slab_tok2[sl]:
                yv.wait_ge(sOPt.h, slab_tok2[sl])
            if tc == 0:
                yv.wait_ge(sYW.h, a1_yw)
            for qtr in range(4):
                src = yT_s[:, qtr * 4:(qtr + 1) * 4, tc * TCK:(tc + 1) * TCK]
                dst = ytb[:, sl * NJT * TCK:(sl + 1) * NJT * TCK].rearrange(
                    "p (j c) -> p j c", c=TCK)[:, qtr * 4:(qtr + 1) * 4, :]
                sYR.inc(yv.dma_start(dst, src), 16)
            slab_need = sYR.n
            op_tok = 0
            for tl in range(4):
                tt = tc * 4 + tl
                xs = xb2[:, (tt % 2) * D:(tt % 2 + 1) * D]
                if x_tok[tt % 2]:
                    yv.wait_ge(sA2v.h, x_tok[tt % 2])
                sX.inc(yv.dma_start(xs, x_d[tt * TTK:(tt + 1) * TTK, :]), 16)
                x_need = sX.n
                tv.wait_ge(sYR.h, slab_need)
                tv.wait_ge(sWq.h, wo_need)
                for mc in range(4):
                    if pout_tok[mc]:
                        tv.wait_ge(sA2v.h, pout_tok[mc])
                    mm = None
                    for c in range(NJT):
                        lhsT = ytb[:, sl * NJT * TCK + c * TCK + tl * 128:
                                   sl * NJT * TCK + c * TCK + (tl + 1) * 128]
                        rhs = wob[:, c * D + mc * 512:c * D + (mc + 1) * 512]
                        mm = tv.matmul(pout[:, mc * 512:(mc + 1) * 512], lhsT, rhs,
                                       start=(c == 0), stop=(c == NJT - 1))
                    op_tok = sOPt.inc(mm, 1)
                    vv.wait_ge(sOPt.h, op_tok)
                    vv.wait_ge(sX.h, x_need)
                    if mc == 0:
                        if x2w_prev:
                            vv.wait_ge(sX2W.h, x2w_prev)
                        if ssq_prev:
                            vv.wait_ge(sA2s.h, ssq_prev)
                    ev = vv.scalar_tensor_tensor(x2st[:, mc * 512:(mc + 1) * 512],
                                                 pout[:, mc * 512:(mc + 1) * 512],
                                                 1.0 - LAMBDA_INIT,
                                                 xs[:, mc * 512:(mc + 1) * 512],
                                                 op0=ALU.mult, op1=ALU.add)
                    pout_tok[mc] = sA2v.inc(ev, 1)
                x_tok[tt % 2] = sA2v.n
                x2_ready = sA2v.n
                yv.wait_ge(sA2v.h, x2_ready)
                sX2W.inc(yv.dma_start(x2_s[tt], x2st[:, :]), 16)
                x2w_prev = sX2W.n
                do_ln2 = a2_sub in (None, "ln2", "router")
                do_rt = a2_sub in (None, "router")
                do_cwt = a2_sub is None
                st = stats[:, (tt % 2) * 8:(tt % 2) * 8 + 8]
                if not do_ln2:
                    continue
                sv.wait_ge(sA2v.h, x2_ready)
                if tt >= 1:
                    sv.wait_ge(sA2v.h, sA2v.n)
                ssq_tok = sA2s.inc(sv.activation(sqs2[:, :], x2st[:, :], AF.Square, accum_out=st[:, 0:1]), 1)
                ssq_prev = ssq_tok
                t_r = sA2v.inc(vv.reduce_sum(st[:, 1:2], x2st[:, :], axis=AX.X), 1)
                vv.wait_ge(sA2s.h, ssq_tok)
                vv.wait_ge(sA2v.h, t_r)
                t_m = sA2v.inc(vv.tensor_scalar(st[:, 2:3], st[:, 1:2], 1.0 / D, None, op0=ALU.mult), 1)
                sA2v.inc(vv.tensor_scalar(st[:, 3:4], st[:, 0:1], 1.0 / D, None, op0=ALU.mult), 1)
                vv.wait_ge(sA2v.h, t_m)
                t_n = sA2v.inc(vv.tensor_scalar(st[:, 4:5], st[:, 2:3], st[:, 2:3], -1.0, op0=ALU.mult, op1=ALU.mult), 1)
                vv.wait_ge(sA2v.h, t_n)
                var_tok = sA2v.inc(vv.tensor_add(st[:, 3:4], st[:, 3:4], st[:, 4:5]), 1)
                sv.wait_ge(sA2v.h, var_tok)
                sA2s.inc(sv.activation(st[:, 5:6], st[:, 3:4], AF.Ln, bias=eps_ln[:, 0:1]), 1)
                rstd_tok = sA2s.inc(sv.activation(st[:, 6:7], st[:, 5:6], AF.Exp, scale=-0.5), 1)
                vv.wait_ge(sA2s.h, rstd_tok)
                if tt >= 1:
                    vv.wait_ge(sRt.h, ln2_tr_prev)
                ln2_prev = sA2v.inc(vv.tensor_scalar(ln2b[:, :], x2st[:, :], st[:, 2:3], st[:, 6:7],
                                                     op0=ALU.subtract, op1=ALU.mult), 1)
                ln2_toks[tt] = ln2_prev
                tv.wait_ge(sA2v.h, ln2_prev)
                for b in range(2):
                    if st2_tok[b]:
                        tv.wait_ge(sA2v.h, st2_tok[b])
                    ti = None
                    for k in range(8):
                        jt = b * 8 + k
                        ti = tv.transpose(pt2[:, b * 1024 + k * 128:b * 1024 + (k + 1) * 128],
                                          ln2b[:, jt * 128:(jt + 1) * 128], identb[:, :])
                    tr_tok = sRt.inc(ti, 1)
                    vv.wait_ge(sRt.h, tr_tok)
                    if tl == 0 and b == 0 and flush2_tok:
                        vv.wait_ge(sL2W.h, flush2_tok)
                    dst = stage2[:, :].rearrange("p (j q c) -> p j q c", q=4, c=128)[:, b * 8:(b + 1) * 8, tl, :]
                    src = pt2[:, b * 1024:(b + 1) * 1024].rearrange("p (j c) -> p j c", c=128)
                    st2_tok[b] = sA2v.inc(vv.tensor_copy(dst, src), 1)
                ln2_tr_prev = sRt.n
                stage_tok = sA2v.n
                if not do_rt:
                    continue
                tv.wait_ge(sA2v.h, stage_tok)
                mm = None
                for j in range(NJT):
                    lhsT = stage2[:, :].rearrange("p (j q c) -> p j q c", q=4, c=128)[:, j, tl, :]
                    mm = tv.matmul(pscr[:, 0:8], lhsT, rwb[:, j * 8:(j + 1) * 8],
                                   start=(j == 0), stop=(j == NJT - 1))
                sc_tok = sRt.inc(mm, 1)
                sv.wait_ge(sRt.h, sc_tok)
                exp_tok = sA2s.inc(sv.activation(rsc[:, 8:16], pscr[:, 0:8], AF.Exp), 1)
                vv.wait_ge(sA2s.h, exp_tok)
                t_sum = sA2v.inc(vv.reduce_sum(rsc[:, 1:2], rsc[:, 8:16], axis=AX.X), 1)
                sv.wait_ge(sA2v.h, t_sum)
                sv.activation(rsc[:, 2:3], rsc[:, 1:2], AF.Ln)
                t_rs = sA2s.inc(sv.activation(rsc[:, 2:3], rsc[:, 2:3], AF.Exp, scale=-1.0), 1)
                vv.wait_ge(sA2s.h, t_rs)
                t_p = sA2v.inc(vv.tensor_scalar(rsc[:, 8:16], rsc[:, 8:16], rsc[:, 2:3], None, op0=ALU.mult), 1)
                vv.wait_ge(sA2v.h, t_p)
                t_m1 = sA2v.inc(vv.reduce_max(rsc[:, 3:4], rsc[:, 8:16], axis=AX.X), 1)
                vv.wait_ge(sA2v.h, t_m1)
                t_k1 = sA2v.inc(vv.tensor_scalar(rsc[:, 16:24], rsc[:, 8:16], rsc[:, 3:4], None, op0=ALU.is_ge), 1)
                vv.wait_ge(sA2v.h, t_k1)
                t_pm = sA2v.inc(vv.scalar_tensor_tensor(rsc[:, 16:24], rsc[:, 16:24], -1e30, rsc[:, 8:16],
                                                        op0=ALU.mult, op1=ALU.add), 1)
                vv.wait_ge(sA2v.h, t_pm)
                t_m2 = sA2v.inc(vv.reduce_max(rsc[:, 4:5], rsc[:, 16:24], axis=AX.X), 1)
                vv.wait_ge(sA2v.h, t_m2)
                t_dn = sA2v.inc(vv.tensor_add(rsc[:, 5:6], rsc[:, 3:4], rsc[:, 4:5]), 1)
                sv.wait_ge(sA2v.h, t_dn)
                sv.activation(rsc[:, 6:7], rsc[:, 5:6], AF.Ln)
                t_rd = sA2s.inc(sv.activation(rsc[:, 6:7], rsc[:, 6:7], AF.Exp, scale=-1.0), 1)
                t_k2 = sA2v.inc(vv.tensor_scalar(rsc[:, 16:24], rsc[:, 8:16], rsc[:, 4:5], None, op0=ALU.is_ge), 1)
                vv.wait_ge(sA2v.h, t_k2)
                t_pk = sA2v.inc(vv.tensor_mul(rsc[:, 8:16], rsc[:, 8:16], rsc[:, 16:24]), 1)
                vv.wait_ge(sA2v.h, t_pk)
                vv.wait_ge(sA2s.h, t_rd)
                sA2v.inc(vv.tensor_scalar(cw_sb[:, tt * 8:(tt + 1) * 8], rsc[:, 8:16],
                                          rsc[:, 6:7], None, op0=ALU.mult), 1)
                if not do_cwt:
                    continue
                emit_cw_transpose()
                pend_cw[0] = (tt, sA2v.n)
            slab_tok2[sl] = op_tok
            yv.wait_ge(sA2v.h, sA2v.n)
            for half in range(4):
                src = stage2[:, :].rearrange("p (j c) -> p j c", c=TCK)[:, half * 4:(half + 1) * 4, :]
                sL2W.inc(yv.dma_start(ln2T_s[:, half * 4:(half + 1) * 4, tc * TCK:(tc + 1) * TCK], src), 16)
            flush2_tok = sL2W.n
        emit_cw_transpose()
        yv.wait_ge(sCWv.h, sCWv.n)
        sCWw.inc(yv.dma_start(cwT_s[0:8, :], cwT_full[0:8, :]), 16)
        if cw_dbg is not None:
            yv.wait_ge(sA2v.h, sA2v.n)
            sOW.inc(yv.dma_start(cw_dbg, cw_sb[:, :]), 16)
        a2_done = dict(l2w=sL2W.n, x2w=sX2W.n, cww=sCWw.n)

    if stop_after == "a2":
        yv.wait_ge(sL2W.h, sL2W.n)
        yv.wait_ge(sX2W.h, sX2W.n)
        yv.wait_ge(sCWw.h, sCWw.n)
        yv.wait_ge(sP0s.h, (0 if skip_p0 else 16 * len(p0_jobs)))
        finish_early()
        return nc

    if debug_outputs and "out" not in debug_outputs:
        yv.wait_ge(sL1W.h, sL1W.n)
        yv.wait_ge(sYW.h, sYW.n)
        yv.wait_ge(sL2W.h, sL2W.n)
        yv.wait_ge(sX2W.h, sX2W.n)
        yv.wait_ge(sP0s.h, (0 if skip_p0 else 16 * len(p0_jobs)))
        yv.wait_ge(sOW.h, sOW.n)
        return nc

    p0_ctx.close()

    with (
        nc.sbuf_tensor("l2b", [128, NJT * TCK], BF16) as l2b,
        nc.sbuf_tensor("actb", [128, E * NFT * 512], BF16) as actb,
        nc.sbuf_tensor("gupb", [128, 2 * NJT * 256], BF16) as gupb,
        nc.sbuf_tensor("dwnb", [128, 2 * NFT * 1024], BF16) as dwnb,
        nc.sbuf_tensor("cwbb", [128, 2 * TCK], F32) as cwbb,
        nc.sbuf_tensor("cwT8b", [8, 2 * TCK], F32) as cwT8b,
        nc.sbuf_tensor("x2rb", [128, 4 * 1024], F32) as x2rb,
        nc.sbuf_tensor("outst", [128, 4 * 1024], F32) as outst,
        nc.sbuf_tensor("silub", [128, 512], F32) as silub,
        nc.sbuf_tensor("acttmp", [128, 512], F32) as acttmp,
    ):
        gw_tok = [0, 0]
        dw_tok = [0, 0]
        l2_tok = 0
        x2r_tok = 0
        out_tok = 0
        b2t_prev = 0
        sv.wait_ge(sP0s.h, (0 if skip_p0 else 16 * len(p0_jobs)))
        for tc in range(NTC):
            if l2_tok:
                yv.wait_ge(sB1t.h, l2_tok)
            if tc == 0:
                yv.wait_ge(sL2W.h, a2_done["l2w"])
            for qtr in range(4):
                src = ln2T_s[:, qtr * 4:(qtr + 1) * 4, tc * TCK:(tc + 1) * TCK]
                dst = l2b[:, :].rearrange("p (j c) -> p j c", c=TCK)[:, qtr * 4:(qtr + 1) * 4, :]
                sL2R.inc(yv.dma_start(dst, src), 16)
            l2_need = sL2R.n
            if tc == 0:
                yv.wait_ge(sCWw.h, a2_done["cww"])
            if tc >= 2:
                yv.wait_ge(sCBt.h, sCBt.n)
            sCTR.inc(yv.dma_start(cwT8b[0:8, (tc % 2) * TCK:(tc % 2 + 1) * TCK],
                                  cwT_s[0:8, tc * TCK:(tc + 1) * TCK]), 16)
            ct_need = sCTR.n

            with (
                nc.psum_tensor(f"pgu{tc}", [128, 4 * 512], F32) as pgu,
                nc.psum_tensor(f"pcwb{tc}", [128, 512], F32) as pcwb,
            ):
                pgu_s_tok = [0, 0]
                pgu_v_tok = [0, 0]
                silu_v_tok = 0
                for e in range(E):
                    tv.wait_ge(sCTR.h, ct_need)
                    if sCBv.n:
                        tv.wait_ge(sCBv.h, sCBv.n)
                    cb = tv.matmul(pcwb[:, :], sel[0:8, e * 128:(e + 1) * 128],
                                   cwT8b[0:8, (tc % 2) * TCK:(tc % 2 + 1) * TCK], start=True, stop=True)
                    cb_tok = sCBt.inc(cb, 1)
                    vv.wait_ge(sCBt.h, cb_tok)
                    if e >= 2:
                        pass
                    cwb_ready = sCBv.inc(vv.tensor_copy(cwbb[:, (e % 2) * TCK:(e % 2 + 1) * TCK], pcwb[:, :]), 1)
                    for ft in range(NFT):
                        gslot = (e * NFT + ft) % 2
                        qe, qs = (sv, sGW) if ft % 2 == 0 else (yv, sGW2)
                        if gw_tok[gslot]:
                            qe.wait_ge(sB1t.h, gw_tok[gslot])
                        qs.inc(qe.dma_start(gupb[:, gslot * NJT * 256:(gslot + 1) * NJT * 256],
                                            gupb_s[e, ft]), 16)
                        g_need = (qs, qs.n)
                        par = ft % 2
                        gb, ub = 2 * par, 2 * par + 1
                        if pgu_s_tok[par]:
                            tv.wait_ge(sB1s.h, pgu_s_tok[par])
                        if pgu_v_tok[par]:
                            tv.wait_ge(sB1v.h, pgu_v_tok[par])
                        tv.wait_ge(g_need[0].h, g_need[1])
                        tv.wait_ge(sL2R.h, l2_need)
                        mm = None
                        for j in range(NJT):
                            base = gslot * NJT * 256 + j * 256
                            lnt = l2b[:, j * TCK:(j + 1) * TCK]
                            tv.matmul(pgu[:, gb * 512:(gb + 1) * 512],
                                      gupb[:, base:base + 128], lnt,
                                      start=(j == 0), stop=(j == NJT - 1))
                            mm = tv.matmul(pgu[:, ub * 512:(ub + 1) * 512],
                                           gupb[:, base + 128:base + 256], lnt,
                                           start=(j == 0), stop=(j == NJT - 1))
                        mm_tok = sB1t.inc(mm, 1)
                        gw_tok[gslot] = mm_tok
                        sv.wait_ge(sB1t.h, mm_tok)
                        if silu_v_tok:
                            sv.wait_ge(sB1v.h, silu_v_tok)
                        si = sv.activation(silub[:, :], pgu[:, gb * 512:(gb + 1) * 512], AF.Silu)
                        si_tok = sB1s.inc(si, 1)
                        pgu_s_tok[par] = si_tok
                        vv.wait_ge(sB1s.h, si_tok)
                        if b2t_prev:
                            vv.wait_ge(sB2t.h, b2t_prev)
                        t1 = vv.tensor_mul(acttmp[:, :], silub[:, :], pgu[:, ub * 512:(ub + 1) * 512])
                        up_tok = sB1v.inc(t1, 1)
                        silu_v_tok = up_tok
                        pgu_v_tok[par] = up_tok
                        vv.wait_ge(sCBv.h, cwb_ready)
                        vv.wait_ge(sB1v.h, up_tok)
                        t2 = vv.tensor_mul(actb[:, (e * NFT + ft) * 512:(e * NFT + ft + 1) * 512],
                                           acttmp[:, :], cwbb[:, (e % 2) * TCK:(e % 2 + 1) * TCK])
                        sB1v.inc(t2, 1)
                l2_tok = sB1t.n
                b1_v_final = sB1v.n

            with nc.psum_tensor(f"pdn{tc}", [128, 8 * 512], F32) as pdn:
                for dh in range(2):
                    if x2r_tok:
                        yv.wait_ge(sB2v.h, x2r_tok)
                    if tc == 0 and dh == 0:
                        yv.wait_ge(sX2W.h, a2_done["x2w"])
                    src = x2_s[tc * 4:(tc + 1) * 4, :, dh * 1024:(dh + 1) * 1024].rearrange("t p c -> p t c")
                    dst = x2rb[:, :].rearrange("p (t c) -> p t c", c=1024)
                    sX2R.inc(yv.dma_start(dst, src), 16)
                    x2_need = sX2R.n
                    for e in range(E):
                        dslot = e % 2
                        qe, qs = (sv, sDW) if e % 2 == 0 else (yv, sDW2)
                        if dw_tok[dslot]:
                            qe.wait_ge(sB2t.h, dw_tok[dslot])
                        qs.inc(qe.dma_start(dwnb[:, dslot * NFT * 1024:(dslot + 1) * NFT * 1024],
                                            dwnb_s[e, dh]), 16)
                        tv.wait_ge(qs.h, qs.n)
                        if e == 0:
                            tv.wait_ge(sB1v.h, b1_v_final)
                            tv.wait_ge(sB2v.h, sB2v.n)
                        mm = None
                        for f in range(NFT):
                            for tl2 in (3, 2, 1, 0):
                                lhsT = actb[:, (e * NFT + f) * 512 + tl2 * 128:
                                            (e * NFT + f) * 512 + (tl2 + 1) * 128]
                                for dc in range(2):
                                    bank = tl2 * 2 + dc
                                    mm = tv.matmul(pdn[:, bank * 512:(bank + 1) * 512],
                                                   lhsT,
                                                   dwnb[:, dslot * NFT * 1024 + f * 1024 + dc * 512:
                                                        dslot * NFT * 1024 + f * 1024 + (dc + 1) * 512],
                                                   start=(e == 0 and f == 0), stop=(e == E - 1 and f == NFT - 1))
                        dw_tok[dslot] = sB2t.inc(mm, 1)
                    b2t_prev = sB2t.n
                    vv.wait_ge(sB2t.h, b2t_prev)
                    vv.wait_ge(sX2R.h, x2_need)
                    if out_tok:
                        vv.wait_ge(sOW.h, out_tok)
                    for tl2 in range(4):
                        for dc in range(2):
                            bank = tl2 * 2 + dc
                            ee = vv.tensor_add(outst[:, tl2 * 1024 + dc * 512:tl2 * 1024 + (dc + 1) * 512],
                                               pdn[:, bank * 512:(bank + 1) * 512],
                                               x2rb[:, tl2 * 1024 + dc * 512:tl2 * 1024 + (dc + 1) * 512])
                            sB2v.inc(ee, 1)
                    x2r_tok = sB2v.n
                    yv.wait_ge(sB2v.h, sB2v.n)
                    for tl2 in range(4):
                        tt = tc * 4 + tl2
                        sOW.inc(yv.dma_start(out_d[tt * TTK:(tt + 1) * TTK, dh * 1024:(dh + 1) * 1024],
                                             outst[:, tl2 * 1024:(tl2 + 1) * 1024]), 16)
                    out_tok = sOW.n

    yv.wait_ge(sOW.h, sOW.n)
    return nc


def _prep_shared(i):
    f32 = np.float32
    wq = np.asarray(i["L0_attn_wq"], f32)
    wk = np.asarray(i["L0_attn_wk"], f32)
    wv = np.asarray(i["L0_attn_wv"], f32)
    wo = np.asarray(i["L0_attn_wo"], f32)
    rw = np.asarray(i["L0_ffn_router_w"], f32)
    gate = np.asarray(i["L0_ffn_gate_proj"], f32)
    up = np.asarray(i["L0_ffn_up_proj"], f32)
    dwn = np.asarray(i["L0_ffn_down_proj"], f32)

    wqkvT = np.stack([
        np.concatenate([wq[h * 256:(h + 1) * 256].T,
                        wk[h * 256:(h + 1) * 256].T,
                        wv[h * 256:(h + 1) * 256].T], axis=1)
        for h in range(H)
    ])
    wqkvT = np.ascontiguousarray(
        wqkvT.reshape(H, NJT, 128, 768).transpose(0, 2, 1, 3)).reshape(H, 128, NJT * 768)

    woT = np.ascontiguousarray(wo.T.reshape(NJT, 128, D).transpose(1, 0, 2)).reshape(128, NJT * D)
    rwt = np.ascontiguousarray(rw.reshape(NJT, 128, 8).transpose(1, 0, 2)).reshape(128, NJT * 8)

    g = gate.reshape(E, NJT, 128, NFT, 128).transpose(0, 3, 2, 1, 4)
    u = up.reshape(E, NJT, 128, NFT, 128).transpose(0, 3, 2, 1, 4)
    gup = np.ascontiguousarray(np.concatenate([g, u], axis=4)).reshape(E, NFT, 128, NJT * 256)

    dw = dwn.reshape(E, NFT, 128, 2, 1024).transpose(0, 3, 2, 1, 4)
    dw = np.ascontiguousarray(dw).reshape(E, 2, 128, NFT * 1024)

    lamv = np.concatenate([np.asarray(i["L0_attn_lq1"], f32), np.asarray(i["L0_attn_lk1"], f32),
                           np.asarray(i["L0_attn_lq2"], f32), np.asarray(i["L0_attn_lk2"], f32)]).reshape(1, 1024)

    k_idx = np.arange(128)[:, None]
    q_idx = np.arange(TCK)[None, :]
    masksm = np.concatenate([(q_idx >= k_idx + 128 * o).astype(f32)
                             for o in range(4)], axis=1)

    identf = np.eye(128, dtype=f32)
    selm = np.zeros((8, 8 * 128), f32)
    for e in range(E):
        selm[e, e * 128:(e + 1) * 128] = 1.0
    ones32 = np.ones((1, 128), f32)

    return {"wqkvT": wqkvT, "woT": woT, "rw": rwt, "gup": gup, "dwn": dw,
            "lam_d": lamv, "masks_d": masksm,
            "identf_d": identf, "sel_d": selm, "ones32_d": ones32}


def kernel(**inputs):
    x = np.asarray(inputs["L0_attn_x"], np.float32)
    shared = _prep_shared(inputs)
    nc = build_program()
    in_maps = []
    for c in range(8):
        m = dict(shared)
        m["x"] = np.ascontiguousarray(x[c])
        in_maps.append(m)
    res = run_bass_kernel_spmd(nc, in_maps, core_ids=list(range(8)))
    return np.stack([res.results[c]["out"] for c in range(8)])


# version 07 (final)
# speedup vs baseline: 1.2171x; 1.1965x over previous
from contextlib import ExitStack

import numpy as np
import ml_dtypes
import concourse.bass as bass
import concourse.mybir as mybir
from concourse.bass_utils import run_bass_kernel_spmd

F32 = mybir.dt.float32
BF16 = mybir.dt.bfloat16
AF = mybir.ActivationFunctionType
ALU = mybir.AluOpType
AX = mybir.AxisListType

T, D, H = 4096, 2048, 8
NTT, TTK = 32, 128
NTC, TCK = 8, 512
NJT = 16
E, FF, NFT = 8, 1024, 8
VS = 264
SCALE = float(1.0 / np.sqrt(128.0))
EXPSHIFT = -11.5
LN_EPS = 1e-5
RMS_EPS = 1.1920929e-07
LAMBDA_INIT = 0.8


class Sem:
    def __init__(self, nc, name):
        self.h = nc.alloc_semaphore(name=name)
        self.n = 0

    def inc(self, instr, k=1):
        instr.then_inc(self.h, k)
        self.n += k
        return self.n


def build_program(debug_outputs=(), stop_after=None, a2_sub=None, skip_p0=False):
    nc = bass.Bass("TRN2", target_bir_lowering=False)

    def dk(name):
        return "ExternalOutput" if name in debug_outputs else "Internal"

    x_d = nc.dram_tensor("x", [T, D], F32, kind="ExternalInput").ap()
    wqkv_d = nc.dram_tensor("wqkvT", [H, 128, NJT * 768], F32, kind="ExternalInput").ap()
    wo_d = nc.dram_tensor("woT", [128, NJT * D], F32, kind="ExternalInput").ap()
    rw_d = nc.dram_tensor("rw", [128, NJT * 8], F32, kind="ExternalInput").ap()
    gup_d = nc.dram_tensor("gup", [E, NFT, 128, NJT * 256], F32, kind="ExternalInput").ap()
    dwn_d = nc.dram_tensor("dwn", [E, 2, 128, NFT * 1024], F32, kind="ExternalInput").ap()
    lam_d = nc.dram_tensor("lam_d", [1, 1024], F32, kind="ExternalInput").ap()
    masks_d = nc.dram_tensor("masks_d", [128, 4 * TCK], F32, kind="ExternalInput").ap()
    identf_d = nc.dram_tensor("identf_d", [128, 128], F32, kind="ExternalInput").ap()
    sel_d = nc.dram_tensor("sel_d", [8, 8 * 128], F32, kind="ExternalInput").ap()
    ones_d = nc.dram_tensor("ones32_d", [1, 128], F32, kind="ExternalInput").ap()

    out_d = nc.dram_tensor("out", [T, D], F32, kind="ExternalOutput").ap()

    ln1T_s = nc.dram_tensor("ln1T_s", [128, NJT, T], BF16, kind=dk("ln1T_s")).ap()
    yT_s = nc.dram_tensor("yT_s", [128, NJT, T], BF16, kind=dk("yT_s")).ap()
    ln2T_s = nc.dram_tensor("ln2T_s", [128, NJT, T], BF16, kind=dk("ln2T_s")).ap()
    x2_s = nc.dram_tensor("x2_s", [NTT, 128, D], F32, kind=dk("x2_s")).ap()
    gupb_s = nc.dram_tensor("gupb_s", [E, NFT, 128, NJT * 256], BF16, kind="Internal").ap()
    cwT_s = nc.dram_tensor("cwT_s", [8, T], F32, kind=dk("cwT_s")).ap()
    wqkvb_s = nc.dram_tensor("wqkvb_s", [H, 128, NJT * 768], BF16, kind=dk("wqkvb_s")).ap()
    wob_s = nc.dram_tensor("wob_s", [128, NJT * D], BF16, kind="Internal").ap()
    rwb_s = nc.dram_tensor("rwb_s", [128, NJT * 8], BF16, kind="Internal").ap()
    dwnb_s = nc.dram_tensor("dwnb_s", [E, 2, 128, NFT * 1024], BF16, kind="Internal").ap()
    cw_dbg = None
    if "cw" in debug_outputs:
        cw_dbg = nc.dram_tensor("cw", [128, NTT * 8], F32, kind="ExternalOutput").ap()

    identb = nc.alloc_sbuf_tensor("identb", [128, 128], BF16)
    identf = nc.alloc_sbuf_tensor("identf", [128, 128], F32)
    masks = nc.alloc_sbuf_tensor("masks", [128, 4 * TCK], BF16)
    sel = nc.alloc_sbuf_tensor("sel", [8, 8 * 128], F32)
    ones32 = nc.alloc_sbuf_tensor("ones32", [1, 128], F32)
    lam = nc.alloc_sbuf_tensor("lam", [1, 1024], F32)
    lprod = nc.alloc_sbuf_tensor("lprod", [1, 512], F32)
    lsc = nc.alloc_sbuf_tensor("lsc", [1, 8], F32)
    nlam = nc.alloc_sbuf_tensor("nlam", [128, 1], F32)
    bias_exp = nc.alloc_sbuf_tensor("bias_exp", [128, 1], F32)
    eps_ln = nc.alloc_sbuf_tensor("eps_ln", [128, 1], F32)
    eps_rms = nc.alloc_sbuf_tensor("eps_rms", [128, 1], F32)
    cw_sb = nc.alloc_sbuf_tensor("cw_sb", [128, NTT * 8], F32)
    stats = nc.alloc_sbuf_tensor("stats", [128, 16], F32)
    rmsst = nc.alloc_sbuf_tensor("rmsst", [128, 16], F32)

    sems = {}
    def S(name):
        if name not in sems:
            sems[name] = Sem(nc, name)
        return sems[name]

    (sCn, sP0l, sP0s, sX, sA0s, sA0v, sA0t, sL1W, sL1R, sWq, sQKt, sQKs, sQKv,
     sTt, sTv, sSTt, sSTs, sMv, sPVt, sYv, sYTt, sYTv, sYW, sYR, sOPt, sA2v,
     sA2s, sRt, sCWt, sCWv, sL2W, sX2W, sL2R, sX2R, sGW, sDW, sCBt, sCBv,
     sB1t, sB1s, sB1v, sB2t, sB2v, sOW, sLamv, sLams, sLamt, sCWw, sCTR, sYs, sGW2, sDW2) = [S(n) for n in (
        "sCn sP0l sP0s sX sA0s sA0v sA0t sL1W sL1R sWq sQKt sQKs sQKv sTt sTv "
        "sSTt sSTs sMv sPVt sYv sYTt sYTv sYW sYR sOPt sA2v sA2s sRt sCWt sCWv "
        "sL2W sX2W sL2R sX2R sGW sDW sCBt sCBv sB1t sB1s sB1v sB2t sB2v sOW "
        "sLamv sLams sLamt sCWw sCTR sYs sGW2 sDW2").split()]

    tv, vv, sv, gv, yv = nc.tensor, nc.vector, nc.scalar, nc.gpsimd, nc.sync

    sCn.inc(yv.dma_start(identf[:, :], identf_d), 16)
    sCn.inc(yv.dma_start(sel[:, :], sel_d), 16)
    sCn.inc(yv.dma_start(ones32[:, :], ones_d), 16)
    sCn.inc(yv.dma_start(lam[:, :], lam_d), 16)
    sCn.inc(gv.memset(bias_exp[:, :], EXPSHIFT), 1)
    sCn.inc(gv.memset(eps_ln[:, :], LN_EPS), 1)
    sCn.inc(gv.memset(eps_rms[:, :], RMS_EPS), 1)
    sCn.inc(gv.memset(stats[:, :], 0.0), 1)
    sCn.inc(gv.memset(rmsst[:, :], 0.0), 1)
    with nc.sbuf_tensor("mstage", [128, 4 * TCK], F32) as mstage:
        ml_tok = Sem(nc, "sMst")
        ml_tok.inc(yv.dma_start(mstage[:, :], masks_d), 16)
        vv.wait_ge(ml_tok.h, ml_tok.n)
        sCn.inc(vv.tensor_copy(masks[:, :], mstage[:, :]), 1)
        vv.wait_ge(sCn.h, 16)
        sCn.inc(vv.tensor_copy(identb[:, :], identf[:, :]), 1)
    cn_target = sCn.n

    with nc.psum_tensor("plam", [128, 1], F32) as plam:
        vv.wait_ge(sCn.h, cn_target)
        t1 = sLamv.inc(vv.tensor_mul(lprod[0:1, 0:256], lam[0:1, 0:256], lam[0:1, 256:512]), 1)
        t2 = sLamv.inc(vv.tensor_mul(lprod[0:1, 256:512], lam[0:1, 512:768], lam[0:1, 768:1024]), 1)
        vv.wait_ge(sLamv.h, t2)
        sLamv.inc(vv.reduce_sum(lsc[0:1, 0:1], lprod[0:1, 0:256], axis=AX.X), 1)
        sLamv.inc(vv.reduce_sum(lsc[0:1, 1:2], lprod[0:1, 256:512], axis=AX.X), 1)
        sv.wait_ge(sLamv.h, sLamv.n)
        sLams.inc(sv.activation(lsc[0:1, 4:6], lsc[0:1, 0:2], AF.Exp), 1)
        vv.wait_ge(sLams.h, sLams.n)
        t3 = sLamv.inc(vv.tensor_sub(lsc[0:1, 6:7], lsc[0:1, 5:6], lsc[0:1, 4:5]), 1)
        vv.wait_ge(sLamv.h, t3)
        sLamv.inc(vv.tensor_scalar(lsc[0:1, 6:7], lsc[0:1, 6:7], -LAMBDA_INIT, None, op0=ALU.add), 1)
        tv.wait_ge(sLamv.h, sLamv.n)
        sLamt.inc(tv.matmul(plam[:, :], ones32[0:1, :], lsc[0:1, 6:7], start=True, stop=True), 1)
        vv.wait_ge(sLamt.h, sLamt.n)
        sLamv.inc(vv.tensor_copy(nlam[:, :], plam[:, :]), 1)
    lam_v_final = sLamv.n

    JW = 2048
    p0_jobs = []
    for h in range(H):
        for c in range(6):
            p0_jobs.append((wqkv_d[h][:, c * JW:(c + 1) * JW], wqkvb_s[h][:, c * JW:(c + 1) * JW]))
    p0_wq_job = [6 * (h + 1) for h in range(H)]
    for c in range(16):
        p0_jobs.append((wo_d[:, c * JW:(c + 1) * JW], wob_s[:, c * JW:(c + 1) * JW]))
    p0_jobs.append((rw_d, rwb_s))
    p0_attn_jobs = len(p0_jobs)
    for e in range(E):
        for ft in range(NFT):
            for c in range(2):
                p0_jobs.append((gup_d[e, ft][:, c * JW:(c + 1) * JW],
                                gupb_s[e, ft][:, c * JW:(c + 1) * JW]))
    for e in range(E):
        for dh in range(2):
            for c in range(4):
                p0_jobs.append((dwn_d[e, dh][:, c * JW:(c + 1) * JW],
                                dwnb_s[e, dh][:, c * JW:(c + 1) * JW]))
    p0_ctx = ExitStack()
    p0_f32 = p0_ctx.enter_context(nc.sbuf_tensor("p0f", [128, 2 * JW], F32))
    p0_bf = p0_ctx.enter_context(nc.sbuf_tensor("p0b", [128, 2 * JW], BF16))
    sP0c = Sem(nc, "sP0c")
    p0_state = {"i": 0}

    def emit_p0(k):
        if skip_p0:
            return
        for _ in range(k):
            i = p0_state["i"]
            if i >= len(p0_jobs):
                return
            srcap, dstap = p0_jobs[i]
            w = srcap.shape[-1]
            sl_ = i % 2
            if i == 0:
                yv.wait_ge(sCn.h, cn_target)
            if i >= 2:
                yv.wait_ge(sP0c.h, i - 1)
            sP0l.inc(yv.dma_start(p0_f32[:, sl_ * JW:sl_ * JW + w], srcap), 16)
            sv.wait_ge(sP0l.h, sP0l.n)
            if i >= 2:
                sv.wait_ge(sP0s.h, 16 * (i - 1))
            sP0c.inc(sv.copy(p0_bf[:, sl_ * JW:sl_ * JW + w], p0_f32[:, sl_ * JW:sl_ * JW + w]), 1)
            yv.wait_ge(sP0c.h, sP0c.n)
            sP0s.inc(yv.dma_start(dstap, p0_bf[:, sl_ * JW:sl_ * JW + w]), 16)
            p0_state["i"] += 1

    def finish_early():
        vv.wait_ge(sCn.h, cn_target)
        dummy = sOW
        fi = yv.dma_start(out_d[0:128, 0:128], identf[:, :])
        dummy.inc(fi, 16)
        yv.wait_ge(dummy.h, dummy.n)

    if stop_after == "init":
        emit_p0(len(p0_jobs))
        gv.wait_ge(sP0s.h, (0 if skip_p0 else 16 * len(p0_jobs)))
        finish_early()
        return nc

    with (
        nc.sbuf_tensor("xb", [128, 2 * D], F32) as xb,
        nc.sbuf_tensor("sqs", [128, D], F32) as sqs,
        nc.sbuf_tensor("ln1b", [128, D], BF16) as ln1b,
        nc.sbuf_tensor("stage0", [128, NJT * TCK], BF16) as stage0,
        nc.psum_tensor("pt0", [128, 2 * 1024], BF16) as pt0,
    ):
        emit_p0(40)
        sv.wait_ge(sCn.h, cn_target)
        vv.wait_ge(sCn.h, cn_target)
        tv.wait_ge(sCn.h, cn_target)
        norm_tok = [0] * NTT
        tr_toks = [0] * NTT
        evac_tok = [0, 0]
        flush_tok = 0
        for tt in range(NTT):
            slx = tt % 2
            if tt >= 2:
                yv.wait_ge(sA0v.h, norm_tok[tt - 2])
            sX.inc(yv.dma_start(xb[:, slx * D:(slx + 1) * D], x_d[tt * TTK:(tt + 1) * TTK, :]), 16)
            x_need = sX.n
            xs = xb[:, slx * D:(slx + 1) * D]
            st = stats[:, (tt % 2) * 8:(tt % 2) * 8 + 8]
            sv.wait_ge(sX.h, x_need)
            if tt >= 1:
                sv.wait_ge(sA0v.h, sA0v.n)
            ssq_tok = sA0s.inc(sv.activation(sqs[:, :], xs, AF.Square, accum_out=st[:, 0:1]), 1)
            vv.wait_ge(sX.h, x_need)
            t_r = sA0v.inc(vv.reduce_sum(st[:, 1:2], xs, axis=AX.X), 1)
            vv.wait_ge(sA0s.h, ssq_tok)
            vv.wait_ge(sA0v.h, t_r)
            t_m = sA0v.inc(vv.tensor_scalar(st[:, 2:3], st[:, 1:2], 1.0 / D, None, op0=ALU.mult), 1)
            sA0v.inc(vv.tensor_scalar(st[:, 3:4], st[:, 0:1], 1.0 / D, None, op0=ALU.mult), 1)
            vv.wait_ge(sA0v.h, t_m)
            t_n = sA0v.inc(vv.tensor_scalar(st[:, 4:5], st[:, 2:3], st[:, 2:3], -1.0, op0=ALU.mult, op1=ALU.mult), 1)
            vv.wait_ge(sA0v.h, t_n)
            var_tok = sA0v.inc(vv.tensor_add(st[:, 3:4], st[:, 3:4], st[:, 4:5]), 1)
            sv.wait_ge(sA0v.h, var_tok)
            sA0s.inc(sv.activation(st[:, 5:6], st[:, 3:4], AF.Ln, bias=eps_ln[:, 0:1]), 1)
            rstd_tok = sA0s.inc(sv.activation(st[:, 6:7], st[:, 5:6], AF.Exp, scale=-0.5), 1)
            vv.wait_ge(sA0s.h, rstd_tok)
            if tt >= 1:
                vv.wait_ge(sA0t.h, tr_toks[tt - 1])
            norm_tok[tt] = sA0v.inc(vv.tensor_scalar(ln1b[:, :], xs, st[:, 2:3], st[:, 6:7],
                                                     op0=ALU.subtract, op1=ALU.mult), 1)
            tv.wait_ge(sA0v.h, norm_tok[tt])
            for b in range(2):
                if evac_tok[b]:
                    tv.wait_ge(sA0v.h, evac_tok[b])
                ti = None
                for k in range(8):
                    jt = b * 8 + k
                    ti = tv.transpose(pt0[:, b * 1024 + k * 128:b * 1024 + (k + 1) * 128],
                                      ln1b[:, jt * 128:(jt + 1) * 128], identb[:, :])
                tr_toks[tt] = sA0t.inc(ti, 1)
                vv.wait_ge(sA0t.h, tr_toks[tt])
                if tt % 4 == 0 and b == 0 and flush_tok:
                    vv.wait_ge(sL1W.h, flush_tok)
                dst = stage0[:, :].rearrange("p (j q c) -> p j q c", q=4, c=128)[:, b * 8:(b + 1) * 8, tt % 4, :]
                src = pt0[:, b * 1024:(b + 1) * 1024].rearrange("p (j c) -> p j c", c=128)
                evac_tok[b] = sA0v.inc(vv.tensor_copy(dst, src), 1)
            if tt % 4 == 3:
                tc = tt // 4
                yv.wait_ge(sA0v.h, sA0v.n)
                for half in range(4):
                    src = stage0[:, :].rearrange("p (j c) -> p j c", c=TCK)[:, half * 4:(half + 1) * 4, :]
                    sL1W.inc(yv.dma_start(ln1T_s[:, half * 4:(half + 1) * 4, tc * TCK:(tc + 1) * TCK], src), 16)
                flush_tok = sL1W.n

    if stop_after == "a0":
        yv.wait_ge(sL1W.h, sL1W.n)
        finish_early()
        return nc

    with (
        nc.sbuf_tensor("l1b", [128, 2 * NJT * TCK], BF16) as l1b,
        nc.sbuf_tensor("wqb", [128, 2 * NJT * 768], BF16) as wqb,
        nc.sbuf_tensor("qnT1", [128, T], BF16) as qnT1,
        nc.sbuf_tensor("qnT2", [128, T], BF16) as qnT2,
        nc.sbuf_tensor("knT1", [128, T], BF16) as knT1,
        nc.sbuf_tensor("knT2", [128, T], BF16) as knT2,
        nc.sbuf_tensor("vp", [128, NTT * VS], BF16) as vp,
        nc.sbuf_tensor("pbuf", [128, 4 * TCK], BF16) as pbuf,
        nc.sbuf_tensor("qkst", [128, TCK], BF16) as qkst,
        nc.sbuf_tensor("sqs1", [128, TCK], F32) as sqs1,
        nc.sbuf_tensor("ytmp", [128, 4 * 256], F32) as ytmp,
        nc.sbuf_tensor("ybf", [128, 4 * 256], BF16) as ybf,
        nc.sbuf_tensor("yst", [128, 2 * TCK], BF16) as yst,
        nc.sbuf_tensor("zz", [128, 16], F32) as zz,
    ):
        vones = vp[:, :].rearrange("p (k c) -> p k c", c=VS)[:, :, 256:257]
        vv.wait_ge(sCn.h, cn_target)
        sCn.inc(vv.memset(vones, 1.0), 1)
        vones_tok = sCn.n

        slab_tok = [0, 0]
        wq_tok = [0, 0]
        if not skip_p0:
            yv.wait_ge(sP0s.h, 16 * p0_wq_job[0])
        sWq.inc(yv.dma_start(wqb[:, 0:NJT * 768], wqkvb_s[0]), 16)
        l1_iter = 0
        prev_ev = [0, 0, 0, 0]
        pb_tok = [0, 0, 0, 0]
        pst_tok = [0, 0, 0]
        yst_tok = 0
        ytr_prev = 0

        for h in range(H):
            with (
                nc.psum_tensor(f"pq{h}", [128, 2 * 1024], F32) as pq,
                nc.psum_tensor(f"ptq{h}", [128, 512], BF16) as ptq,
            ):
                if h + 1 < H:
                    if wq_tok[(h + 1) % 2]:
                        yv.wait_ge(sQKt.h, wq_tok[(h + 1) % 2])
                    if not skip_p0:
                        yv.wait_ge(sP0s.h, 16 * p0_wq_job[h + 1])
                    sWq.inc(yv.dma_start(wqb[:, ((h + 1) % 2) * NJT * 768:((h + 1) % 2 + 1) * NJT * 768],
                                         wqkvb_s[h + 1]), 16)
                wq_need = 16 * (h + 1)
                emit_p0(28)
                if h >= 1:
                    tv.wait_ge(sSTs.h, sSTs.n)
                    tv.wait_ge(sYv.h, sYv.n)
                    tv.wait_ge(sYTv.h, sYTv.n)
                    vv.wait_ge(sSTt.h, sSTt.n)
                    vv.wait_ge(sPVt.h, sPVt.n)
                pq_s_tok = [0, 0]
                pq_v_tok = [0, 0]
                ptq_tok = 0
                qk_ev_tok = [0] * NTT
                for tc in range(NTC):
                    sl = l1_iter % 2
                    if l1_iter >= 2:
                        yv.wait_ge(sQKt.h, slab_tok[sl])
                    if l1_iter == 0:
                        yv.wait_ge(sL1W.h, sL1W.n)
                    for qtr in range(4):
                        src = ln1T_s[:, qtr * 4:(qtr + 1) * 4, tc * TCK:(tc + 1) * TCK]
                        dst = l1b[:, sl * NJT * TCK:(sl + 1) * NJT * TCK].rearrange(
                            "p (j c) -> p j c", c=TCK)[:, qtr * 4:(qtr + 1) * 4, :]
                        sL1R.inc(yv.dma_start(dst, src), 16)
                    slab_need = sL1R.n
                    l1_iter += 1
                    qk_tok = 0
                    for tl in range(4):
                        tt = tc * 4 + tl
                        ps = tt % 2
                        tv.wait_ge(sL1R.h, slab_need)
                        tv.wait_ge(sWq.h, wq_need)
                        if pq_s_tok[ps]:
                            tv.wait_ge(sQKs.h, pq_s_tok[ps])
                        if pq_v_tok[ps]:
                            tv.wait_ge(sQKv.h, pq_v_tok[ps])
                        mm = None
                        for j in range(NJT):
                            lhsT = l1b[:, sl * NJT * TCK + j * TCK + tl * 128:
                                       sl * NJT * TCK + j * TCK + (tl + 1) * 128]
                            rhs = wqb[:, (h % 2) * NJT * 768 + j * 768:(h % 2) * NJT * 768 + (j + 1) * 768]
                            st0 = (j == 0)
                            tv.matmul(pq[:, ps * 1024:ps * 1024 + 512], lhsT, rhs[:, 0:512],
                                      start=st0, stop=(j == NJT - 1))
                            mm = tv.matmul(pq[:, ps * 1024 + 512:ps * 1024 + 768], lhsT, rhs[:, 512:768],
                                           start=st0, stop=(j == NJT - 1))
                        qk_tok = sQKt.inc(mm, 1)
                        sv.wait_ge(sQKt.h, qk_tok)
                        if tt >= 1:
                            sv.wait_ge(sQKv.h, qk_ev_tok[tt - 1])
                        ai = None
                        for g in range(4):
                            ai = sv.activation(sqs1[:, g * 128:(g + 1) * 128],
                                               pq[:, ps * 1024 + g * 128:ps * 1024 + (g + 1) * 128],
                                               AF.Square, accum_out=rmsst[:, g:g + 1])
                        pq_s_tok[ps] = sA = sQKs.inc(ai, 1)
                        rcol = 8 + (tt % 2) * 4
                        sQKs.inc(sv.activation(rmsst[:, 4:8], rmsst[:, 0:4], AF.Ln,
                                               bias=eps_rms[:, 0:1], scale=1.0 / 128.0), 1)
                        rstd_tok = sQKs.inc(sv.activation(rmsst[:, rcol:rcol + 4], rmsst[:, 4:8], AF.Exp, scale=-0.5), 1)
                        vv.wait_ge(sQKs.h, rstd_tok)
                        if ptq_tok:
                            vv.wait_ge(sTt.h, sTt.n)
                        ev = None
                        for g in range(4):
                            ev = vv.tensor_scalar(qkst[:, g * 128:(g + 1) * 128],
                                                  pq[:, ps * 1024 + g * 128:ps * 1024 + (g + 1) * 128],
                                                  rmsst[:, rcol + g:rcol + g + 1], None, op0=ALU.mult)
                        qn_tok = sQKv.inc(ev, 1)
                        vv1 = vv.tensor_copy(vp[:, tt * VS:tt * VS + 256], pq[:, ps * 1024 + 512:ps * 1024 + 768])
                        pq_v_tok[ps] = sQKv.inc(vv1, 1)
                        tv.wait_ge(sQKv.h, qn_tok)
                        if ptq_tok:
                            tv.wait_ge(sQKv.h, ptq_tok)
                        ti = None
                        for g in range(4):
                            ti = tv.transpose(ptq[:, g * 128:(g + 1) * 128],
                                              qkst[:, g * 128:(g + 1) * 128], identb[:, :])
                        tr_tok = sTt.inc(ti, 1)
                        vv.wait_ge(sTt.h, tr_tok)
                        e4 = None
                        for gi, dstT in enumerate((qnT1, qnT2, knT1, knT2)):
                            e4 = vv.tensor_copy(dstT[:, tt * 128:(tt + 1) * 128],
                                                ptq[:, gi * 128:(gi + 1) * 128])
                        ptq_tok = sQKv.inc(e4, 1)
                        qk_ev_tok[tt] = ptq_tok
                    slab_tok[sl] = qk_tok
                if h + 1 < H:
                    wq_tok[(h + 1) % 2] = sQKt.n
                wq_tok[h % 2] = sQKt.n
                qkv_v_final = sQKv.n
                qkv_s_final = sQKs.n

            with (
                nc.psum_tensor(f"pst{h}", [128, 3 * TCK], F32) as pst,
                nc.psum_tensor(f"py{h}", [128, 4 * 512], F32) as py,
                nc.psum_tensor(f"pty{h}", [128, 256], BF16) as pty,
            ):
                if h == 0:
                    tv.wait_ge(sCn.h, vones_tok)
                kctr = 0
                for qsc in range(NTC):
                    nkt = 4 * qsc + 4
                    evq = [0, 0, 0, 0]
                    for var in range(2):
                        qn = qnT1 if var == 0 else qnT2
                        kn = knT1 if var == 0 else knT2
                        tv.wait_ge(sQKv.h, qkv_v_final)
                        tv.wait_ge(sQKs.h, qkv_s_final)

                        pend = []

                        def flush_pv(item, first_base, last_kt):
                            kt, pslot, ex_tok, mask_tok = item
                            if mask_tok:
                                tv.wait_ge(sMv.h, mask_tok)
                            else:
                                tv.wait_ge(sSTs.h, ex_tok)
                            mm = None
                            for qt in range(4):
                                if kt == 0 and first_base[qt]:
                                    tv.wait_ge(sYv.h, first_base[qt])
                                mm = tv.matmul(py[:, qt * 512:qt * 512 + 257],
                                               pbuf[:, pslot * TCK + qt * 128:pslot * TCK + (qt + 1) * 128],
                                               vp[:, kt * VS:kt * VS + 257],
                                               start=(kt == 0), stop=(kt == last_kt))
                            pb_tok[pslot] = sPVt.inc(mm, 1)

                        for kt in range(nkt):
                            b = kctr % 3
                            if pst_tok[b]:
                                tv.wait_ge(sSTs.h, pst_tok[b])
                            mm = tv.matmul(pst[:, b * TCK:(b + 1) * TCK],
                                           kn[:, kt * 128:(kt + 1) * 128],
                                           qn[:, qsc * TCK:(qsc + 1) * TCK],
                                           start=True, stop=True)
                            st_tok = sSTt.inc(mm, 1)
                            pslot = kctr % 4
                            sv.wait_ge(sSTt.h, st_tok)
                            if pb_tok[pslot]:
                                sv.wait_ge(sPVt.h, pb_tok[pslot])
                            ex = sv.activation(pbuf[:, pslot * TCK:(pslot + 1) * TCK],
                                               pst[:, b * TCK:(b + 1) * TCK],
                                               AF.Exp, bias=bias_exp[:, 0:1], scale=SCALE)
                            ex_tok = sSTs.inc(ex, 1)
                            pst_tok[b] = ex_tok
                            o = kt - 4 * qsc
                            mask_tok = None
                            if o >= 0:
                                vv.wait_ge(sSTs.h, ex_tok)
                                if pb_tok[pslot]:
                                    vv.wait_ge(sPVt.h, pb_tok[pslot])
                                mk = vv.tensor_mul(pbuf[:, pslot * TCK:(pslot + 1) * TCK],
                                                   pbuf[:, pslot * TCK:(pslot + 1) * TCK],
                                                   masks[:, o * TCK:(o + 1) * TCK])
                                mask_tok = sMv.inc(mk, 1)
                            pend.append((kt, pslot, ex_tok, mask_tok))
                            kctr += 1
                            if len(pend) > 3:
                                flush_pv(pend.pop(0), prev_ev if var == 0 else evq, nkt - 1)
                        while pend:
                            flush_pv(pend.pop(0), prev_ev if var == 0 else evq, nkt - 1)

                        pv_done = sPVt.n
                        sv.wait_ge(sPVt.h, pv_done)
                        if var == 0:
                            zt = None
                            for qt in range(4):
                                sv.activation(zz[:, 8:9], py[:, qt * 512 + 256:qt * 512 + 257], AF.Ln)
                                zt = sv.activation(zz[:, qt:qt + 1], zz[:, 8:9], AF.Exp, scale=-1.0)
                            z_tok = sYs.inc(zt, 1)
                            vv.wait_ge(sYs.h, z_tok)
                            vv.wait_ge(sPVt.h, pv_done)
                            for qt in range(4):
                                e = vv.tensor_scalar(ytmp[:, qt * 256:(qt + 1) * 256],
                                                     py[:, qt * 512:qt * 512 + 256],
                                                     zz[:, qt:qt + 1], None, op0=ALU.mult)
                                evq[qt] = sYv.inc(e, 1)
                        else:
                            zt = None
                            for qt in range(4):
                                sv.activation(zz[:, 8:9], py[:, qt * 512 + 256:qt * 512 + 257], AF.Ln)
                                sv.activation(zz[:, 9:10], zz[:, 8:9], AF.Exp, scale=-1.0)
                                zt = sv.mul(zz[:, 4 + qt:5 + qt], zz[:, 9:10], nlam[:, 0:1])
                            z_tok = sYs.inc(zt, 1)
                            vv.wait_ge(sYs.h, z_tok)
                            vv.wait_ge(sPVt.h, pv_done)
                            if ytr_prev:
                                vv.wait_ge(sYTt.h, ytr_prev)
                            for qt in range(4):
                                e = vv.scalar_tensor_tensor(ybf[:, qt * 256:(qt + 1) * 256],
                                                            py[:, qt * 512:qt * 512 + 256],
                                                            zz[:, 4 + qt:5 + qt],
                                                            ytmp[:, qt * 256:(qt + 1) * 256],
                                                            op0=ALU.mult, op1=ALU.add)
                                evq[qt] = sYv.inc(e, 1)
                            prev_ev = list(evq)
                    ti = None
                    for qt in range(4):
                        tv.wait_ge(sYv.h, evq[qt])
                        tv.wait_ge(sYTv.h, sYTv.n)
                        for ct in range(2):
                            ti = tv.transpose(pty[:, ct * 128:(ct + 1) * 128],
                                              ybf[:, qt * 256 + ct * 128:qt * 256 + (ct + 1) * 128],
                                              identb[:, :])
                        ytok = sYTt.inc(ti, 1)
                        vv.wait_ge(sYTt.h, ytok)
                        if qt == 0 and yst_tok:
                            vv.wait_ge(sYW.h, yst_tok)
                        c1 = vv.tensor_copy(yst[:, qt * 128:(qt + 1) * 128], pty[:, 0:128])
                        c2 = vv.tensor_copy(yst[:, TCK + qt * 128:TCK + (qt + 1) * 128], pty[:, 128:256])
                        sYTv.inc(c1, 1)
                        sYTv.inc(c2, 1)
                    ytr_prev = sYTt.n
                    yv.wait_ge(sYTv.h, sYTv.n)
                    sYW.inc(yv.dma_start(yT_s[:, h * 2, qsc * TCK:(qsc + 1) * TCK], yst[:, 0:TCK]), 16)
                    sYW.inc(yv.dma_start(yT_s[:, h * 2 + 1, qsc * TCK:(qsc + 1) * TCK], yst[:, TCK:2 * TCK]), 16)
                    yst_tok = sYW.n
        if "att" in debug_outputs:
            qd = [nc.dram_tensor(f"qk_dbg{i}", [128, T], BF16, kind="ExternalOutput").ap() for i in range(4)]
            vd = nc.dram_tensor("vp_dbg", [128, NTT * VS], BF16, kind="ExternalOutput").ap()
            zzd = nc.dram_tensor("zz_dbg", [128, 16], F32, kind="ExternalOutput").ap()
            ytd = nc.dram_tensor("ytmp_dbg", [128, 4 * 256], F32, kind="ExternalOutput").ap()
            ybd = nc.dram_tensor("ybf_dbg", [128, 4 * 256], BF16, kind="ExternalOutput").ap()
            pbd = nc.dram_tensor("pbuf_dbg", [128, 4 * TCK], BF16, kind="ExternalOutput").ap()
            sDbg = Sem(nc, "sDbg")
            yv.wait_ge(sYW.h, sYW.n)
            for i, t_ in enumerate((qnT1, qnT2, knT1, knT2)):
                sDbg.inc(yv.dma_start(qd[i], t_[:, :]), 16)
            sDbg.inc(yv.dma_start(vd, vp[:, :]), 16)
            sDbg.inc(yv.dma_start(zzd, zz[:, :]), 16)
            sDbg.inc(yv.dma_start(ytd, ytmp[:, :]), 16)
            sDbg.inc(yv.dma_start(ybd, ybf[:, :]), 16)
            sDbg.inc(yv.dma_start(pbd, pbuf[:, :]), 16)
            mkd = nc.dram_tensor("masks_dbg", [128, 4 * TCK], BF16, kind="ExternalOutput").ap()
            sDbg.inc(yv.dma_start(mkd, masks[:, :]), 16)
            nld = nc.dram_tensor("nlam_dbg", [128, 1], F32, kind="ExternalOutput").ap()
            sDbg.inc(yv.dma_start(nld, nlam[:, :]), 16)
            yv.wait_ge(sDbg.h, sDbg.n)
        a1_yw = sYW.n

    if stop_after == "a1":
        yv.wait_ge(sYW.h, sYW.n)
        finish_early()
        return nc

    with (
        nc.sbuf_tensor("ytb", [128, 2 * NJT * TCK], BF16) as ytb,
        nc.sbuf_tensor("wob", [128, NJT * D], BF16) as wob,
        nc.sbuf_tensor("xb2", [128, 2 * D], F32) as xb2,
        nc.sbuf_tensor("x2st", [128, D], F32) as x2st,
        nc.sbuf_tensor("sqs2", [128, D], F32) as sqs2,
        nc.sbuf_tensor("ln2b", [128, D], BF16) as ln2b,
        nc.sbuf_tensor("stage2", [128, NJT * TCK], BF16) as stage2,
        nc.sbuf_tensor("rwb", [128, NJT * 8], BF16) as rwb,
        nc.sbuf_tensor("cwT_full", [8, T], F32) as cwT_full,
        nc.sbuf_tensor("rsc", [128, 24], F32) as rsc,
        nc.psum_tensor("pout", [128, D], F32) as pout,
        nc.psum_tensor("pscr", [128, 8], F32) as pscr,
        nc.psum_tensor("pt2", [128, 2 * 1024], BF16) as pt2,
        nc.psum_tensor("pcw", [128, 128], F32) as pcw,
    ):
        if not skip_p0:
            yv.wait_ge(sP0s.h, 16 * p0_attn_jobs)
        sWq.inc(yv.dma_start(wob[:, :], wob_s), 16)
        sWq.inc(yv.dma_start(rwb[:, :], rwb_s), 16)
        wo_need = sWq.n
        pend_cw = [None]

        def emit_cw_transpose():
            if pend_cw[0] is None:
                return
            ptt, ptok = pend_cw[0]
            pend_cw[0] = None
            tv.wait_ge(sA2v.h, ptok)
            if sCWv.n:
                tv.wait_ge(sCWv.h, sCWv.n)
            ct_tok = sCWt.inc(tv.matmul(pcw[0:8, 0:128], cw_sb[:, ptt * 8:(ptt + 1) * 8],
                                        identf[:, :], start=True, stop=True), 1)
            vv.wait_ge(sCWt.h, ct_tok)
            sCWv.inc(vv.tensor_copy(cwT_full[0:8, ptt * 128:(ptt + 1) * 128], pcw[0:8, 0:128]), 1)
        emit_p0(len(p0_jobs))
        assert skip_p0 or p0_state["i"] == len(p0_jobs)

        slab_tok2 = [0, 0]
        x_tok = [0, 0]
        pout_tok = [0, 0, 0, 0]
        st2_tok = [0, 0]
        flush2_tok = 0
        x2w_prev = 0
        ln2_prev = 0
        ln2_tr_prev = 0
        ssq_prev = 0
        ln2_toks = [0] * NTT
        for tc in range(NTC):
            sl = tc % 2
            if slab_tok2[sl]:
                yv.wait_ge(sOPt.h, slab_tok2[sl])
            if tc == 0:
                yv.wait_ge(sYW.h, a1_yw)
            for qtr in range(4):
                src = yT_s[:, qtr * 4:(qtr + 1) * 4, tc * TCK:(tc + 1) * TCK]
                dst = ytb[:, sl * NJT * TCK:(sl + 1) * NJT * TCK].rearrange(
                    "p (j c) -> p j c", c=TCK)[:, qtr * 4:(qtr + 1) * 4, :]
                sYR.inc(yv.dma_start(dst, src), 16)
            slab_need = sYR.n
            op_tok = 0
            for tl in range(4):
                tt = tc * 4 + tl
                xs = xb2[:, (tt % 2) * D:(tt % 2 + 1) * D]
                if x_tok[tt % 2]:
                    yv.wait_ge(sA2v.h, x_tok[tt % 2])
                sX.inc(yv.dma_start(xs, x_d[tt * TTK:(tt + 1) * TTK, :]), 16)
                x_need = sX.n
                tv.wait_ge(sYR.h, slab_need)
                tv.wait_ge(sWq.h, wo_need)
                for mc in range(4):
                    if pout_tok[mc]:
                        tv.wait_ge(sA2v.h, pout_tok[mc])
                    mm = None
                    for c in range(NJT):
                        lhsT = ytb[:, sl * NJT * TCK + c * TCK + tl * 128:
                                   sl * NJT * TCK + c * TCK + (tl + 1) * 128]
                        rhs = wob[:, c * D + mc * 512:c * D + (mc + 1) * 512]
                        mm = tv.matmul(pout[:, mc * 512:(mc + 1) * 512], lhsT, rhs,
                                       start=(c == 0), stop=(c == NJT - 1))
                    op_tok = sOPt.inc(mm, 1)
                    vv.wait_ge(sOPt.h, op_tok)
                    vv.wait_ge(sX.h, x_need)
                    if mc == 0:
                        if x2w_prev:
                            vv.wait_ge(sX2W.h, x2w_prev)
                        if ssq_prev:
                            vv.wait_ge(sA2s.h, ssq_prev)
                    ev = vv.scalar_tensor_tensor(x2st[:, mc * 512:(mc + 1) * 512],
                                                 pout[:, mc * 512:(mc + 1) * 512],
                                                 1.0 - LAMBDA_INIT,
                                                 xs[:, mc * 512:(mc + 1) * 512],
                                                 op0=ALU.mult, op1=ALU.add)
                    pout_tok[mc] = sA2v.inc(ev, 1)
                x_tok[tt % 2] = sA2v.n
                x2_ready = sA2v.n
                yv.wait_ge(sA2v.h, x2_ready)
                sX2W.inc(yv.dma_start(x2_s[tt], x2st[:, :]), 16)
                x2w_prev = sX2W.n
                do_ln2 = a2_sub in (None, "ln2", "router")
                do_rt = a2_sub in (None, "router")
                do_cwt = a2_sub is None
                st = stats[:, (tt % 2) * 8:(tt % 2) * 8 + 8]
                if not do_ln2:
                    continue
                sv.wait_ge(sA2v.h, x2_ready)
                if tt >= 1:
                    sv.wait_ge(sA2v.h, sA2v.n)
                ssq_tok = sA2s.inc(sv.activation(sqs2[:, :], x2st[:, :], AF.Square, accum_out=st[:, 0:1]), 1)
                ssq_prev = ssq_tok
                t_r = sA2v.inc(vv.reduce_sum(st[:, 1:2], x2st[:, :], axis=AX.X), 1)
                vv.wait_ge(sA2s.h, ssq_tok)
                vv.wait_ge(sA2v.h, t_r)
                t_m = sA2v.inc(vv.tensor_scalar(st[:, 2:3], st[:, 1:2], 1.0 / D, None, op0=ALU.mult), 1)
                sA2v.inc(vv.tensor_scalar(st[:, 3:4], st[:, 0:1], 1.0 / D, None, op0=ALU.mult), 1)
                vv.wait_ge(sA2v.h, t_m)
                t_n = sA2v.inc(vv.tensor_scalar(st[:, 4:5], st[:, 2:3], st[:, 2:3], -1.0, op0=ALU.mult, op1=ALU.mult), 1)
                vv.wait_ge(sA2v.h, t_n)
                var_tok = sA2v.inc(vv.tensor_add(st[:, 3:4], st[:, 3:4], st[:, 4:5]), 1)
                sv.wait_ge(sA2v.h, var_tok)
                sA2s.inc(sv.activation(st[:, 5:6], st[:, 3:4], AF.Ln, bias=eps_ln[:, 0:1]), 1)
                rstd_tok = sA2s.inc(sv.activation(st[:, 6:7], st[:, 5:6], AF.Exp, scale=-0.5), 1)
                vv.wait_ge(sA2s.h, rstd_tok)
                if tt >= 1:
                    vv.wait_ge(sRt.h, ln2_tr_prev)
                ln2_prev = sA2v.inc(vv.tensor_scalar(ln2b[:, :], x2st[:, :], st[:, 2:3], st[:, 6:7],
                                                     op0=ALU.subtract, op1=ALU.mult), 1)
                ln2_toks[tt] = ln2_prev
                tv.wait_ge(sA2v.h, ln2_prev)
                for b in range(2):
                    if st2_tok[b]:
                        tv.wait_ge(sA2v.h, st2_tok[b])
                    ti = None
                    for k in range(8):
                        jt = b * 8 + k
                        ti = tv.transpose(pt2[:, b * 1024 + k * 128:b * 1024 + (k + 1) * 128],
                                          ln2b[:, jt * 128:(jt + 1) * 128], identb[:, :])
                    tr_tok = sRt.inc(ti, 1)
                    vv.wait_ge(sRt.h, tr_tok)
                    if tl == 0 and b == 0 and flush2_tok:
                        vv.wait_ge(sL2W.h, flush2_tok)
                    dst = stage2[:, :].rearrange("p (j q c) -> p j q c", q=4, c=128)[:, b * 8:(b + 1) * 8, tl, :]
                    src = pt2[:, b * 1024:(b + 1) * 1024].rearrange("p (j c) -> p j c", c=128)
                    st2_tok[b] = sA2v.inc(vv.tensor_copy(dst, src), 1)
                ln2_tr_prev = sRt.n
                stage_tok = sA2v.n
                if not do_rt:
                    continue
                tv.wait_ge(sA2v.h, stage_tok)
                mm = None
                for j in range(NJT):
                    lhsT = stage2[:, :].rearrange("p (j q c) -> p j q c", q=4, c=128)[:, j, tl, :]
                    mm = tv.matmul(pscr[:, 0:8], lhsT, rwb[:, j * 8:(j + 1) * 8],
                                   start=(j == 0), stop=(j == NJT - 1))
                sc_tok = sRt.inc(mm, 1)
                sv.wait_ge(sRt.h, sc_tok)
                exp_tok = sA2s.inc(sv.activation(rsc[:, 8:16], pscr[:, 0:8], AF.Exp), 1)
                vv.wait_ge(sA2s.h, exp_tok)
                t_sum = sA2v.inc(vv.reduce_sum(rsc[:, 1:2], rsc[:, 8:16], axis=AX.X), 1)
                sv.wait_ge(sA2v.h, t_sum)
                sv.activation(rsc[:, 2:3], rsc[:, 1:2], AF.Ln)
                t_rs = sA2s.inc(sv.activation(rsc[:, 2:3], rsc[:, 2:3], AF.Exp, scale=-1.0), 1)
                vv.wait_ge(sA2s.h, t_rs)
                t_p = sA2v.inc(vv.tensor_scalar(rsc[:, 8:16], rsc[:, 8:16], rsc[:, 2:3], None, op0=ALU.mult), 1)
                vv.wait_ge(sA2v.h, t_p)
                t_m1 = sA2v.inc(vv.reduce_max(rsc[:, 3:4], rsc[:, 8:16], axis=AX.X), 1)
                vv.wait_ge(sA2v.h, t_m1)
                t_k1 = sA2v.inc(vv.tensor_scalar(rsc[:, 16:24], rsc[:, 8:16], rsc[:, 3:4], None, op0=ALU.is_ge), 1)
                vv.wait_ge(sA2v.h, t_k1)
                t_pm = sA2v.inc(vv.scalar_tensor_tensor(rsc[:, 16:24], rsc[:, 16:24], -1e30, rsc[:, 8:16],
                                                        op0=ALU.mult, op1=ALU.add), 1)
                vv.wait_ge(sA2v.h, t_pm)
                t_m2 = sA2v.inc(vv.reduce_max(rsc[:, 4:5], rsc[:, 16:24], axis=AX.X), 1)
                vv.wait_ge(sA2v.h, t_m2)
                t_dn = sA2v.inc(vv.tensor_add(rsc[:, 5:6], rsc[:, 3:4], rsc[:, 4:5]), 1)
                sv.wait_ge(sA2v.h, t_dn)
                sv.activation(rsc[:, 6:7], rsc[:, 5:6], AF.Ln)
                t_rd = sA2s.inc(sv.activation(rsc[:, 6:7], rsc[:, 6:7], AF.Exp, scale=-1.0), 1)
                t_k2 = sA2v.inc(vv.tensor_scalar(rsc[:, 16:24], rsc[:, 8:16], rsc[:, 4:5], None, op0=ALU.is_ge), 1)
                vv.wait_ge(sA2v.h, t_k2)
                t_pk = sA2v.inc(vv.tensor_mul(rsc[:, 8:16], rsc[:, 8:16], rsc[:, 16:24]), 1)
                vv.wait_ge(sA2v.h, t_pk)
                vv.wait_ge(sA2s.h, t_rd)
                sA2v.inc(vv.tensor_scalar(cw_sb[:, tt * 8:(tt + 1) * 8], rsc[:, 8:16],
                                          rsc[:, 6:7], None, op0=ALU.mult), 1)
                if not do_cwt:
                    continue
                emit_cw_transpose()
                pend_cw[0] = (tt, sA2v.n)
            slab_tok2[sl] = op_tok
            yv.wait_ge(sA2v.h, sA2v.n)
            for half in range(4):
                src = stage2[:, :].rearrange("p (j c) -> p j c", c=TCK)[:, half * 4:(half + 1) * 4, :]
                sL2W.inc(yv.dma_start(ln2T_s[:, half * 4:(half + 1) * 4, tc * TCK:(tc + 1) * TCK], src), 16)
            flush2_tok = sL2W.n
        emit_cw_transpose()
        yv.wait_ge(sCWv.h, sCWv.n)
        sCWw.inc(yv.dma_start(cwT_s[0:8, :], cwT_full[0:8, :]), 16)
        if cw_dbg is not None:
            yv.wait_ge(sA2v.h, sA2v.n)
            sOW.inc(yv.dma_start(cw_dbg, cw_sb[:, :]), 16)
        a2_done = dict(l2w=sL2W.n, x2w=sX2W.n, cww=sCWw.n)

    if stop_after == "a2":
        yv.wait_ge(sL2W.h, sL2W.n)
        yv.wait_ge(sX2W.h, sX2W.n)
        yv.wait_ge(sCWw.h, sCWw.n)
        yv.wait_ge(sP0s.h, (0 if skip_p0 else 16 * len(p0_jobs)))
        finish_early()
        return nc

    if debug_outputs and "out" not in debug_outputs:
        yv.wait_ge(sL1W.h, sL1W.n)
        yv.wait_ge(sYW.h, sYW.n)
        yv.wait_ge(sL2W.h, sL2W.n)
        yv.wait_ge(sX2W.h, sX2W.n)
        yv.wait_ge(sP0s.h, (0 if skip_p0 else 16 * len(p0_jobs)))
        yv.wait_ge(sOW.h, sOW.n)
        return nc

    p0_ctx.close()

    with (
        nc.sbuf_tensor("l2b", [128, NJT * TCK], BF16) as l2b,
        nc.sbuf_tensor("actb", [128, E * NFT * 512], BF16) as actb,
        nc.sbuf_tensor("gupb", [128, 4 * NJT * 256], BF16) as gupb,
        nc.sbuf_tensor("dwnb", [128, 2 * NFT * 1024], BF16) as dwnb,
        nc.sbuf_tensor("cwbb", [128, 2 * TCK], F32) as cwbb,
        nc.sbuf_tensor("cwT8b", [8, 2 * TCK], F32) as cwT8b,
        nc.sbuf_tensor("x2rb", [128, 4 * 1024], F32) as x2rb,
        nc.sbuf_tensor("outst", [128, 4 * 1024], F32) as outst,
        nc.sbuf_tensor("silub", [128, 512], F32) as silub,
        nc.sbuf_tensor("acttmp", [128, 512], F32) as acttmp,
    ):
        gw_tok = [0, 0, 0, 0]
        dw_tok = [0, 0]
        l2_tok = 0
        x2r_tok = 0
        out_tok = 0
        b2t_prev = 0
        sv.wait_ge(sP0s.h, (0 if skip_p0 else 16 * len(p0_jobs)))
        for tc in range(NTC):
            if l2_tok:
                yv.wait_ge(sB1t.h, l2_tok)
            if tc == 0:
                yv.wait_ge(sL2W.h, a2_done["l2w"])
            for qtr in range(4):
                src = ln2T_s[:, qtr * 4:(qtr + 1) * 4, tc * TCK:(tc + 1) * TCK]
                dst = l2b[:, :].rearrange("p (j c) -> p j c", c=TCK)[:, qtr * 4:(qtr + 1) * 4, :]
                sL2R.inc(yv.dma_start(dst, src), 16)
            l2_need = sL2R.n
            if tc == 0:
                yv.wait_ge(sCWw.h, a2_done["cww"])
            if tc >= 2:
                yv.wait_ge(sCBt.h, sCBt.n)
            sCTR.inc(yv.dma_start(cwT8b[0:8, (tc % 2) * TCK:(tc % 2 + 1) * TCK],
                                  cwT_s[0:8, tc * TCK:(tc + 1) * TCK]), 16)
            ct_need = sCTR.n

            with (
                nc.psum_tensor(f"pgu{tc}", [128, 4 * 512], F32) as pgu,
                nc.psum_tensor(f"pcwb{tc}", [128, 512], F32) as pcwb,
            ):
                pgu_s_tok = [0, 0]
                pgu_v_tok = [0, 0]
                silu_v_tok = 0
                g_need_tab = {}

                def issue_gup(n):
                    if n >= E * NFT:
                        return
                    e_, ft_ = divmod(n, NFT)
                    gsl = n % 4
                    qe, qs = (sv, sGW) if n % 2 == 0 else (yv, sGW2)
                    if gw_tok[gsl]:
                        qe.wait_ge(sB1t.h, gw_tok[gsl])
                    qs.inc(qe.dma_start(gupb[:, gsl * NJT * 256:(gsl + 1) * NJT * 256],
                                        gupb_s[e_, ft_]), 16)
                    g_need_tab[n] = (qs, qs.n)

                issue_gup(0)
                issue_gup(1)
                for e in range(E):
                    tv.wait_ge(sCTR.h, ct_need)
                    if sCBv.n:
                        tv.wait_ge(sCBv.h, sCBv.n)
                    cb = tv.matmul(pcwb[:, :], sel[0:8, e * 128:(e + 1) * 128],
                                   cwT8b[0:8, (tc % 2) * TCK:(tc % 2 + 1) * TCK], start=True, stop=True)
                    cb_tok = sCBt.inc(cb, 1)
                    vv.wait_ge(sCBt.h, cb_tok)
                    if e >= 2:
                        pass
                    cwb_ready = sCBv.inc(vv.tensor_copy(cwbb[:, (e % 2) * TCK:(e % 2 + 1) * TCK], pcwb[:, :]), 1)
                    for ft in range(NFT):
                        n_it = e * NFT + ft
                        issue_gup(n_it + 2)
                        gslot = n_it % 4
                        g_need = g_need_tab[n_it]
                        par = ft % 2
                        gb, ub = 2 * par, 2 * par + 1
                        if pgu_s_tok[par]:
                            tv.wait_ge(sB1s.h, pgu_s_tok[par])
                        if pgu_v_tok[par]:
                            tv.wait_ge(sB1v.h, pgu_v_tok[par])
                        tv.wait_ge(g_need[0].h, g_need[1])
                        tv.wait_ge(sL2R.h, l2_need)
                        mm = None
                        for j in range(NJT):
                            base = gslot * NJT * 256 + j * 256
                            lnt = l2b[:, j * TCK:(j + 1) * TCK]
                            tv.matmul(pgu[:, gb * 512:(gb + 1) * 512],
                                      gupb[:, base:base + 128], lnt,
                                      start=(j == 0), stop=(j == NJT - 1))
                            mm = tv.matmul(pgu[:, ub * 512:(ub + 1) * 512],
                                           gupb[:, base + 128:base + 256], lnt,
                                           start=(j == 0), stop=(j == NJT - 1))
                        mm_tok = sB1t.inc(mm, 1)
                        gw_tok[gslot] = mm_tok
                        sv.wait_ge(sB1t.h, mm_tok)
                        if silu_v_tok:
                            sv.wait_ge(sB1v.h, silu_v_tok)
                        si = sv.activation(silub[:, :], pgu[:, gb * 512:(gb + 1) * 512], AF.Silu)
                        si_tok = sB1s.inc(si, 1)
                        pgu_s_tok[par] = si_tok
                        vv.wait_ge(sB1s.h, si_tok)
                        if b2t_prev:
                            vv.wait_ge(sB2t.h, b2t_prev)
                        t1 = vv.tensor_mul(acttmp[:, :], silub[:, :], pgu[:, ub * 512:(ub + 1) * 512])
                        up_tok = sB1v.inc(t1, 1)
                        silu_v_tok = up_tok
                        pgu_v_tok[par] = up_tok
                        vv.wait_ge(sCBv.h, cwb_ready)
                        vv.wait_ge(sB1v.h, up_tok)
                        t2 = vv.tensor_mul(actb[:, (e * NFT + ft) * 512:(e * NFT + ft + 1) * 512],
                                           acttmp[:, :], cwbb[:, (e % 2) * TCK:(e % 2 + 1) * TCK])
                        sB1v.inc(t2, 1)
                l2_tok = sB1t.n
                b1_v_final = sB1v.n

            with nc.psum_tensor(f"pdn{tc}", [128, 8 * 512], F32) as pdn:
                for dh in range(2):
                    if x2r_tok:
                        yv.wait_ge(sB2v.h, x2r_tok)
                    if tc == 0 and dh == 0:
                        yv.wait_ge(sX2W.h, a2_done["x2w"])
                    src = x2_s[tc * 4:(tc + 1) * 4, :, dh * 1024:(dh + 1) * 1024].rearrange("t p c -> p t c")
                    dst = x2rb[:, :].rearrange("p (t c) -> p t c", c=1024)
                    sX2R.inc(yv.dma_start(dst, src), 16)
                    x2_need = sX2R.n
                    for e in range(E):
                        dslot = e % 2
                        qe, qs = (sv, sDW) if e % 2 == 0 else (yv, sDW2)
                        if dw_tok[dslot]:
                            qe.wait_ge(sB2t.h, dw_tok[dslot])
                        qs.inc(qe.dma_start(dwnb[:, dslot * NFT * 1024:(dslot + 1) * NFT * 1024],
                                            dwnb_s[e, dh]), 16)
                        tv.wait_ge(qs.h, qs.n)
                        if e == 0:
                            tv.wait_ge(sB1v.h, b1_v_final)
                            tv.wait_ge(sB2v.h, sB2v.n)
                        mm = None
                        for f in range(NFT):
                            for tl2 in (3, 2, 1, 0):
                                lhsT = actb[:, (e * NFT + f) * 512 + tl2 * 128:
                                            (e * NFT + f) * 512 + (tl2 + 1) * 128]
                                for dc in range(2):
                                    bank = tl2 * 2 + dc
                                    mm = tv.matmul(pdn[:, bank * 512:(bank + 1) * 512],
                                                   lhsT,
                                                   dwnb[:, dslot * NFT * 1024 + f * 1024 + dc * 512:
                                                        dslot * NFT * 1024 + f * 1024 + (dc + 1) * 512],
                                                   start=(e == 0 and f == 0), stop=(e == E - 1 and f == NFT - 1))
                        dw_tok[dslot] = sB2t.inc(mm, 1)
                    b2t_prev = sB2t.n
                    vv.wait_ge(sB2t.h, b2t_prev)
                    vv.wait_ge(sX2R.h, x2_need)
                    if out_tok:
                        vv.wait_ge(sOW.h, out_tok)
                    for tl2 in range(4):
                        for dc in range(2):
                            bank = tl2 * 2 + dc
                            ee = vv.tensor_add(outst[:, tl2 * 1024 + dc * 512:tl2 * 1024 + (dc + 1) * 512],
                                               pdn[:, bank * 512:(bank + 1) * 512],
                                               x2rb[:, tl2 * 1024 + dc * 512:tl2 * 1024 + (dc + 1) * 512])
                            sB2v.inc(ee, 1)
                    x2r_tok = sB2v.n
                    yv.wait_ge(sB2v.h, sB2v.n)
                    for tl2 in range(4):
                        tt = tc * 4 + tl2
                        sOW.inc(yv.dma_start(out_d[tt * TTK:(tt + 1) * TTK, dh * 1024:(dh + 1) * 1024],
                                             outst[:, tl2 * 1024:(tl2 + 1) * 1024]), 16)
                    out_tok = sOW.n

    yv.wait_ge(sOW.h, sOW.n)
    return nc


def _prep_shared(i):
    f32 = np.float32
    wq = np.asarray(i["L0_attn_wq"], f32)
    wk = np.asarray(i["L0_attn_wk"], f32)
    wv = np.asarray(i["L0_attn_wv"], f32)
    wo = np.asarray(i["L0_attn_wo"], f32)
    rw = np.asarray(i["L0_ffn_router_w"], f32)
    gate = np.asarray(i["L0_ffn_gate_proj"], f32)
    up = np.asarray(i["L0_ffn_up_proj"], f32)
    dwn = np.asarray(i["L0_ffn_down_proj"], f32)

    wqkvT = np.stack([
        np.concatenate([wq[h * 256:(h + 1) * 256].T,
                        wk[h * 256:(h + 1) * 256].T,
                        wv[h * 256:(h + 1) * 256].T], axis=1)
        for h in range(H)
    ])
    wqkvT = np.ascontiguousarray(
        wqkvT.reshape(H, NJT, 128, 768).transpose(0, 2, 1, 3)).reshape(H, 128, NJT * 768)

    woT = np.ascontiguousarray(wo.T.reshape(NJT, 128, D).transpose(1, 0, 2)).reshape(128, NJT * D)
    rwt = np.ascontiguousarray(rw.reshape(NJT, 128, 8).transpose(1, 0, 2)).reshape(128, NJT * 8)

    g = gate.reshape(E, NJT, 128, NFT, 128).transpose(0, 3, 2, 1, 4)
    u = up.reshape(E, NJT, 128, NFT, 128).transpose(0, 3, 2, 1, 4)
    gup = np.ascontiguousarray(np.concatenate([g, u], axis=4)).reshape(E, NFT, 128, NJT * 256)

    dw = dwn.reshape(E, NFT, 128, 2, 1024).transpose(0, 3, 2, 1, 4)
    dw = np.ascontiguousarray(dw).reshape(E, 2, 128, NFT * 1024)

    lamv = np.concatenate([np.asarray(i["L0_attn_lq1"], f32), np.asarray(i["L0_attn_lk1"], f32),
                           np.asarray(i["L0_attn_lq2"], f32), np.asarray(i["L0_attn_lk2"], f32)]).reshape(1, 1024)

    k_idx = np.arange(128)[:, None]
    q_idx = np.arange(TCK)[None, :]
    masksm = np.concatenate([(q_idx >= k_idx + 128 * o).astype(f32)
                             for o in range(4)], axis=1)

    identf = np.eye(128, dtype=f32)
    selm = np.zeros((8, 8 * 128), f32)
    for e in range(E):
        selm[e, e * 128:(e + 1) * 128] = 1.0
    ones32 = np.ones((1, 128), f32)

    return {"wqkvT": wqkvT, "woT": woT, "rw": rwt, "gup": gup, "dwn": dw,
            "lam_d": lamv, "masks_d": masksm,
            "identf_d": identf, "sel_d": selm, "ones32_d": ones32}


def kernel(**inputs):
    x = np.asarray(inputs["L0_attn_x"], np.float32)
    shared = _prep_shared(inputs)
    nc = build_program()
    in_maps = []
    for c in range(8):
        m = dict(shared)
        m["x"] = np.ascontiguousarray(x[c])
        in_maps.append(m)
    res = run_bass_kernel_spmd(nc, in_maps, core_ids=list(range(8)))
    return np.stack([res.results[c]["out"] for c in range(8)])
